# version 02
# baseline (speedup 1.0000x reference)
import contextlib
import numpy as np
import ml_dtypes
import bass_rust
import concourse.bass as bass
import concourse.mybir as mybir
from concourse.bass import IndirectOffsetOnAxis
from concourse.bass_utils import run_bass_kernel_spmd

F32 = mybir.dt.float32
BF16 = mybir.dt.bfloat16
I32 = mybir.dt.int32
AF = mybir.ActivationFunctionType
ALU = mybir.AluOpType
AX = mybir.AxisListType

NC = 8
T = 1024
D = 2048
H = 5632
S = 8192
DC = 16
HC = 44
KCN = 64
ECN = 16
HEADS = 32
PAIRS = 16
TH = 512
HCP = 6
EPS = 1e-5
HW = 2816
HWC = 22

STAGE = 3
DEBUG = False
DEBUG_OUT = {}


def _dprime(c, qt, kc_rel):
    kc_abs = (kc_rel + 8 * c) % 64
    return 128 * kc_abs - (1024 * c + 512 * qt) + 511


def _geometry():
    mult = []
    for qt in range(2):
        for kc in range(64):
            if any(-127 < _dprime(c, qt, kc) < 1533 for c in range(8)):
                mult.append((qt, kc))
    return mult, {t: i for i, t in enumerate(mult)}


MULT, MIDX = _geometry()
NBAND = len(MULT)
NP_P = 1024
NP_S = 128
NTAB = NP_P + NP_S + NBAND * 640
BAND0 = NP_P + NP_S


def _host_rel_table(rel_pos, c):
    idx = np.zeros(NTAB, dtype=np.int64)
    idx[:NP_P] = np.minimum(np.arange(NP_P), 1022)
    for qt in range(2):
        for kc in range(64):
            d = _dprime(c, qt, kc)
            idx[NP_P + qt * 64 + kc] = 0 if d <= -127 else 1022
    for m, (qt, kc) in enumerate(MULT):
        d = _dprime(c, qt, kc)
        w = np.arange(640)
        idx[BAND0 + 640 * m: BAND0 + 640 * (m + 1)] = np.clip(d + 127 - w, 0, 1022)
    return np.ascontiguousarray(rel_pos[idx].T)


def _host_idx_tables(c):
    p = np.arange(128)
    idx_k = np.zeros((128, 128), np.int32)
    for u in range(8):
        rot = (u + c) % 8
        for j in range(PAIRS):
            idx_k[:, u * 16 + j] = rot * 2048 + 128 * j + p
    idx_v = np.zeros((128, 64), np.int32)
    for kc in range(64):
        kc_abs = (kc + 8 * c) % 64
        idx_v[:, kc] = kc_abs * 128 + p
    return np.concatenate([idx_k, idx_v], axis=1)


def fview(t, off, dims):
    a = t[:]
    return bass_rust.AP(a.tensor, a.offset + off, [list(a.ap[0])] + dims)


def dview(t, off, dims):
    a = t.ap()
    return bass_rust.AP(a.tensor, off, dims)


class KB:
    ENGS = ("sync", "gpsimd", "vector", "scalar", "tensor")

    def __init__(self, nc):
        self.nc = nc
        self.st = {e: [] for e in self.ENGS}
        self.sem = {}
        self.cnt = {}

    def new_sem(self, stack, name):
        self.sem[name] = stack.enter_context(self.nc.semaphore(name))
        self.cnt[name] = 0

    def op(self, eng, f, inc=None, wait=None):
        tgt = None
        if inc is not None:
            self.cnt[inc[0]] += inc[1]
            tgt = self.cnt[inc[0]]
        waits = [(n, t) for (n, t) in (wait or []) if t > 0]
        sems = self.sem

        def g(e, f=f, waits=waits, inc=inc):
            for wn, wt in waits:
                e.wait_ge(sems[wn], wt)
            ins = f(e)
            if inc is not None and ins is not None:
                ins.then_inc(sems[inc[0]], inc[1])
        self.st[eng].append(g)
        return tgt

    def wait(self, eng, name, tgt):
        sems = self.sem
        if tgt > 0:
            self.st[eng].append(lambda e, n=name, t=tgt: e.wait_ge(sems[n], t))


def build_nc():
    nc = bass.Bass("TRN2", target_bir_lowering=False)
    stack = contextlib.ExitStack()

    x_in = nc.dram_tensor("x_in", [T, D], F32, kind="ExternalInput")
    wg_sl = nc.dram_tensor("wg_sl", [HCP * 128, D], I32, kind="ExternalInput")
    wu_sl = nc.dram_tensor("wu_sl", [HCP * 128, D], I32, kind="ExternalInput")
    wd_sl = nc.dram_tensor("wd_sl", [256, H], I32, kind="ExternalInput")
    sg_sl = nc.dram_tensor("sg_sl", [128, HCP], F32, kind="ExternalInput")
    su_sl = nc.dram_tensor("su_sl", [128, HCP], F32, kind="ExternalInput")
    sd_sl = nc.dram_tensor("sd_sl", [128, 2], F32, kind="ExternalInput")
    wa_sl = nc.dram_tensor("wa_sl", [1024, D], F32, kind="ExternalInput")
    rel_tabt = nc.dram_tensor("rel_tabt", [64, NTAB], F32, kind="ExternalInput")
    idx_tab = nc.dram_tensor("idx_tab", [128, 192], I32, kind="ExternalInput")
    ident_f = nc.dram_tensor("ident_f", [128, 128], F32, kind="ExternalInput")
    ident_b = nc.dram_tensor("ident_b", [128, 128], BF16, kind="ExternalInput")
    y_out = nc.dram_tensor("y", [T, D], F32, kind="ExternalOutput")
    if DEBUG:
        dbg_ctx = nc.dram_tensor("dbg_ctx", [T, D], BF16, kind="ExternalOutput")
        dbg_q = nc.dram_tensor("dbg_q", [D, T], BF16, kind="ExternalOutput")
        dbg_kt = nc.dram_tensor("dbg_kt", [NC * D, T], BF16, kind="ExternalOutput")
        dbg_vr = nc.dram_tensor("dbg_vr", [S, D], BF16, kind="ExternalOutput")
        dbg_vin = nc.dram_tensor("dbg_vin", [T, D], BF16, kind="ExternalOutput")
        dbg_acc = nc.dram_tensor("dbg_acc", [128, 520], F32, kind="ExternalOutput")
        dbg_vp = nc.dram_tensor("dbg_vp", [128, KCN * 130], BF16, kind="ExternalOutput")
        dbg_m = nc.dram_tensor("dbg_m", [128, 96], F32, kind="ExternalOutput")
        dbg_kn = nc.dram_tensor("dbg_kn", [NC * HEADS, T], F32, kind="ExternalOutput")
        dbg_bx = nc.dram_tensor("dbg_bx", [NBAND * 640 + 64], BF16, kind="ExternalOutput")

    pgu_out = nc.dram_tensor("pgu_out", [2 * HCP * 128, D], BF16)
    pgu_all = nc.dram_tensor("pgu_all", [NC * 2 * HCP * 128, D], BF16)
    pd_out = nc.dram_tensor("pd_out", [256, H], BF16)
    pd_all = nc.dram_tensor("pd_all", [2048, H], BF16)
    pa_out = nc.dram_tensor("pa_out", [1024, D], BF16)
    pa_all = nc.dram_tensor("pa_all", [8192, D], BF16)
    kt_in = nc.dram_tensor("kt_in", [D, T], BF16)
    kt_out = nc.dram_tensor("kt_out", [NC * D, T], BF16)
    v_in = nc.dram_tensor("v_in", [T, D], BF16)
    v_out = nc.dram_tensor("v_out", [S, D], BF16)
    v_rot = nc.dram_tensor("v_rot", [S, D], BF16)
    kn_in = nc.dram_tensor("kn_in", [HEADS, T], F32)
    kn_out = nc.dram_tensor("kn_out", [NC * HEADS, T], F32)
    qn_dram = nc.dram_tensor("qn_dram", [HEADS, T], F32)
    q_dram = nc.dram_tensor("q_dram", [D, T], BF16)
    bexp_dram = nc.dram_tensor("bexp_dram", [NBAND * 640 + 64], BF16)
    ctx_dram = nc.dram_tensor("ctx_dram", [T, D], BF16)

    k = KB(nc)
    for s in ("sq", "cc", "T", "V", "S", "wsA", "wsB", "kp", "vp",
              "icl", "msc", "xA", "xB", "pA", "pB", "psA2", "psB2", "rl", "bx",
              "ksA", "ksB", "vtA", "vtB", "nst", "vrA", "vrB", "gkb",
              "ctA", "ctB", "cxA", "cxB", "oqA", "oqB", "idx"):
        k.new_sem(stack, s)

    sb = nc.sbuf_tensor
    XT = [stack.enter_context(sb(f"xt{i}", [128, T], F32)) for i in range(DC)]
    LN1 = [stack.enter_context(sb(f"ln{i}", [128, TH], BF16)) for i in range(DC)]
    idf = stack.enter_context(sb("idf", [128, 128], F32))
    idb = stack.enter_context(sb("idb", [128, 128], BF16))
    ones_col_b = stack.enter_context(sb("ones_col_b", [128, 1], BF16))
    w64_col = stack.enter_context(sb("w64_col", [128, 1], BF16))
    mask_top = stack.enter_context(sb("mask_top", [128, 1], BF16))
    mask_bot = stack.enter_context(sb("mask_bot", [128, 1], BF16))
    ones_row_b = stack.enter_context(sb("ones_row_b", [1, 128], BF16))
    ones_row_f = stack.enter_context(sb("ones_row_f", [1, 128], F32))
    brow_ps = stack.enter_context(sb("brow_ps", [1, NP_P + NP_S], F32))
    bmax_sb = stack.enter_context(sb("bmax_sb", [1, 1], F32))
    rowA = stack.enter_context(sb("rowA", [1, TH], F32))
    rowB = stack.enter_context(sb("rowB", [1, TH], F32))
    rowC = stack.enter_context(sb("rowC", [1, TH], F32))
    rowBF = stack.enter_context(sb("rowBF", [1, TH], BF16))
    rowBF2 = stack.enter_context(sb("rowBF2", [1, TH], BF16))
    xbt = stack.enter_context(sb("xbt", [128, TH], BF16))
    xsq = stack.enter_context(sb("xsq", [128, TH], BF16))
    tmpf = stack.enter_context(sb("tmpf", [128, TH], F32))
    Abc = stack.enter_context(sb("Abc", [128, TH], BF16))
    Bbc = stack.enter_context(sb("Bbc", [128, TH], BF16))

    ps_a = stack.enter_context(nc.psum_tensor([128, 1024], F32))
    ps_b = stack.enter_context(nc.psum_tensor([128, 1024], F32))
    ps_c = stack.enter_context(nc.psum_tensor([128, 1024], F32))
    ps_d = stack.enter_context(nc.psum_tensor([128, 1024], F32))

    def psA(h):
        return ps_a[:, h * 512: h * 512 + TH]

    def psB(h):
        return ps_b[:, h * 512: h * 512 + TH]

    def psC(h, w=TH):
        return ps_c[:, h * 512: h * 512 + w]

    def psD(h, w=TH):
        return ps_d[:, h * 512: h * 512 + w]

    k.op("sync", lambda e: e.dma_start(out=idf[:], in_=ident_f[:, :]), inc=("icl", 16))
    k.op("sync", lambda e: e.dma_start(out=idb[:], in_=ident_b[:, :]), inc=("icl", 16))
    id_sq = k.cnt["icl"]
    for buf, val in ((ones_col_b, 1.0), (w64_col, 1.0 / 64), (ones_row_b, 1.0),
                     (ones_row_f, 1.0)):
        k.op("vector", lambda e, b=buf, v=val: e.memset(b[:], v), inc=("V", 1))
    k.op("vector", lambda e: e.memset(mask_top[:], 0.0), inc=("V", 1))
    k.op("vector", lambda e: e.memset(mask_top[0:64, :], 1.0), inc=("V", 1))
    k.op("vector", lambda e: e.memset(mask_bot[:], 0.0), inc=("V", 1))
    k.op("vector", lambda e: e.memset(mask_bot[64:128, :], 1.0), inc=("V", 1))
    consts_V = k.cnt["V"]

    sc1 = contextlib.ExitStack()
    xstage = [sc1.enter_context(sb(f"xs{i}", [128, D], F32)) for i in range(2)]
    pin = [sc1.enter_context(sb(f"pin{i}", [128, HW], I32)) for i in range(2)]
    pdq = [sc1.enter_context(sb(f"pdq{i}", [128, HW], BF16)) for i in range(2)]
    pstr = [sc1.enter_context(sb(f"pstr{i}", [128, HW], BF16)) for i in range(2)]
    scg = sc1.enter_context(sb("scg", [128, HCP], F32))
    scu = sc1.enter_context(sb("scu", [128, HCP], F32))
    scd = sc1.enter_context(sb("scd", [128, 2], F32))
    relc = sc1.enter_context(sb("relc", [64, 2048], F32))
    relb = sc1.enter_context(sb("relb", [64, 2048], BF16))
    bchunk = sc1.enter_context(sb("bchunk", [1, 2048], F32))
    bexpc = sc1.enter_context(sb("bexpc", [1, 2048], BF16))

    ev_hist = []
    for tt in range(8):
        slot = tt % 2
        xsem = "xA" if slot == 0 else "xB"
        lw = k.op("sync", lambda e, tt=tt, slot=slot:
                  e.dma_start(out=xstage[slot][:], in_=x_in[tt * 128:(tt + 1) * 128, :]),
                  inc=(xsem, 16),
                  wait=[("icl", id_sq)] + ([("V", ev_hist[tt - 2])] if tt >= 2 else []))
        for dc in range(DC):
            h = dc % 2
            tT = k.op("tensor", lambda e, slot=slot, dc=dc, h=h:
                      e.transpose(psC(h, 128), xstage[slot][:, dc * 128:(dc + 1) * 128],
                                  idf[:]),
                      inc=("T", 1),
                      wait=[(xsem, lw)] + ([("V", k.cnt["V"])] if (tt or dc) else []))
            k.op("vector", lambda e, dc=dc, tt=tt, h=h:
                 e.tensor_copy(XT[dc][:, tt * 128:(tt + 1) * 128], psC(h, 128)),
                 inc=("V", 1), wait=[("T", tT)])
        ev_hist.append(k.cnt["V"])

    k.op("sync", lambda e: e.dma_start(out=scg[:], in_=sg_sl[:, :]), inc=("msc", 16))
    k.op("sync", lambda e: e.dma_start(out=scu[:], in_=su_sl[:, :]), inc=("msc", 16))
    k.op("sync", lambda e: e.dma_start(out=scd[:], in_=sd_sl[:, :]), inc=("msc", 16))
    sc_sq = k.cnt["msc"]
    k.op("vector", lambda e: e.tensor_scalar_mul(scd[:], scd[:], 0.5), inc=("V", 1),
         wait=[("msc", sc_sq)])
    prep_sc_V = k.cnt["V"]

    prep_dq = [0, 0]
    prep_st = [0, 0]

    def prep_chunk(load_fn, width, dcw, dequant, sc, sc_col, store_fn, li):
        slot = li % 2
        psem = "pA" if slot == 0 else "pB"
        ssem = "psA2" if slot == 0 else "psB2"
        lw = k.op("sync", lambda e, slot=slot: load_fn(e, slot),
                  inc=(psem, 16), wait=[("V", prep_dq[slot])])
        if dequant:
            dqv = k.op("vector", lambda e, slot=slot:
                       e.tensor_scalar(pdq[slot][:, 0:width], pin[slot][:, 0:width],
                                       -128.0, sc[:, sc_col:sc_col + 1],
                                       ALU.add, ALU.mult),
                       inc=("V", 1), wait=[(psem, lw), ("V", prep_sc_V)])
        else:
            dqv = k.op("vector", lambda e, slot=slot:
                       e.tensor_copy(pdq[slot][:, 0:width],
                                     pin[slot].bitcast(F32)[:, 0:width]),
                       inc=("V", 1), wait=[(psem, lw)])
        prep_dq[slot] = dqv
        for dc in range(dcw):
            h = dc % 2
            tT = k.op("tensor", lambda e, slot=slot, dc=dc, h=h:
                      e.transpose(psD(h, 64).bitcast(BF16),
                                  pdq[slot][:, dc * 128:(dc + 1) * 128], idb[:]),
                      inc=("T", 1), wait=[("V", dqv), ("V", k.cnt["V"])])
            k.op("vector", lambda e, slot=slot, dc=dc, h=h, ssem=ssem:
                 e.tensor_copy(pstr[slot][:, dc * 128:(dc + 1) * 128],
                               psD(h, 64).bitcast(BF16)),
                 inc=("V", 1), wait=[("T", tT), (ssem, prep_st[slot])])
        sv = k.cnt["V"]
        prep_st[slot] = k.op("gpsimd", lambda e, slot=slot: store_fn(e, slot),
                             inc=(ssem, 16), wait=[("V", sv)])

    li = 0
    for l in range(HCP):
        prep_chunk(lambda e, slot, l=l: e.dma_start(
            out=pin[slot][:, 0:2048], in_=wg_sl[l * 128:(l + 1) * 128, :]),
            2048, 16, True, scg, l,
            lambda e, slot, l=l: e.dma_start(
                out=pgu_out[l * 128:(l + 1) * 128, :], in_=pstr[slot][:, 0:2048]), li)
        li += 1
    for l in range(HCP):
        prep_chunk(lambda e, slot, l=l: e.dma_start(
            out=pin[slot][:, 0:2048], in_=wu_sl[l * 128:(l + 1) * 128, :]),
            2048, 16, True, scu, l,
            lambda e, slot, l=l: e.dma_start(
                out=pgu_out[(HCP + l) * 128:(HCP + l + 1) * 128, :],
                in_=pstr[slot][:, 0:2048]), li)
        li += 1
    for l in range(2):
        for hh in range(2):
            prep_chunk(lambda e, slot, l=l, hh=hh: e.dma_start(
                out=pin[slot][:, 0:HW],
                in_=wd_sl[l * 128:(l + 1) * 128, hh * HW:(hh + 1) * HW]),
                HW, HWC, True, scd, l,
                lambda e, slot, l=l, hh=hh: e.dma_start(
                    out=pd_out[l * 128:(l + 1) * 128, hh * HW:(hh + 1) * HW],
                    in_=pstr[slot][:, 0:HW]), li)
            li += 1
    for l in range(8):
        prep_chunk(lambda e, slot, l=l: e.dma_start(
            out=pin[slot].bitcast(F32)[:, 0:2048], in_=wa_sl[l * 128:(l + 1) * 128, :]),
            2048, 16, False, None, 0,
            lambda e, slot, l=l: e.dma_start(
                out=pa_out[l * 128:(l + 1) * 128, :], in_=pstr[slot][:, 0:2048]), li)
        li += 1
    prep_bar = [("psA2", k.cnt["psA2"]), ("psB2", k.cnt["psB2"])]

    cc_gu = k.op("gpsimd", lambda e: e.collective_compute(
        "AllGather", ALU.bypass, replica_groups=[list(range(NC))],
        ins=[pgu_out.ap().opt()], outs=[pgu_all.ap().opt()]),
        inc=("cc", 1), wait=prep_bar)
    cc_d = k.op("gpsimd", lambda e: e.collective_compute(
        "AllGather", ALU.bypass, replica_groups=[list(range(NC))],
        ins=[pd_out.ap().opt()], outs=[pd_all.ap().opt()]),
        inc=("cc", 1))
    cc_a = k.op("gpsimd", lambda e: e.collective_compute(
        "AllGather", ALU.bypass, replica_groups=[list(range(NC))],
        ins=[pa_out.ap().opt()], outs=[pa_all.ap().opt()]),
        inc=("cc", 1))

    rel_cv = rel_mm = rel_exp = rel_dma = 0
    nchunks = (NTAB + 2047) // 2048
    for ci in range(nchunks):
        c0 = ci * 2048
        cw = min(2048, NTAB - c0)
        lw = k.op("sync", lambda e, c0=c0, cw=cw:
                  e.dma_start(out=relc[:, 0:cw], in_=rel_tabt[:, c0:c0 + cw]),
                  inc=("rl", 16), wait=[("V", rel_cv)])
        rel_cv = k.op("vector", lambda e, cw=cw: e.tensor_copy(relb[:, 0:cw], relc[:, 0:cw]),
                      inc=("V", 1), wait=[("rl", lw), ("V", consts_V), ("T", rel_mm)])
        for n0 in range(0, cw, 512):
            nw = min(512, cw - n0)
            h = (n0 // 512) % 2
            rel_mm = k.op("tensor", lambda e, n0=n0, nw=nw, h=h:
                          e.matmul(ps_c[0:1, h * 512: h * 512 + nw],
                                   w64_col[0:64, :], relb[0:64, n0:n0 + nw],
                                   start=True, stop=True),
                          inc=("T", 1), wait=[("V", rel_cv), ("V", k.cnt["V"])])
            k.op("vector", lambda e, n0=n0, nw=nw, h=h:
                 e.tensor_copy(bchunk[:, n0:n0 + nw], ps_c[0:1, h * 512: h * 512 + nw]),
                 inc=("V", 1), wait=[("T", rel_mm), ("S", rel_exp)])
        if c0 < NP_P + NP_S:
            w = min(cw, NP_P + NP_S - c0)
            k.op("vector", lambda e, c0=c0, w=w:
                 e.tensor_copy(brow_ps[:, c0:c0 + w], bchunk[:, 0:w]), inc=("V", 1),
                 wait=[("V", k.cnt["V"] - 1)])
        if c0 + cw > BAND0:
            b0 = max(BAND0, c0)
            rel_exp = k.op("scalar", lambda e, c0=c0, b0=b0, cw=cw:
                           e.activation(out=bexpc[:, b0 - c0:cw], in_=bchunk[:, b0 - c0:cw],
                                        func=AF.Exp),
                           inc=("S", 1), wait=[("V", k.cnt["V"]), ("bx", rel_dma)])
            rel_dma = k.op("gpsimd", lambda e, c0=c0, b0=b0, cw=cw:
                           e.dma_start(out=dview(bexp_dram, b0 - BAND0,
                                                 [[0, 1], [1, c0 + cw - b0]]),
                                       in_=bexpc[0:1, b0 - c0:cw]),
                           inc=("bx", 16), wait=[("S", rel_exp)])
    k.op("vector", lambda e: e.reduce_max(bmax_sb[:], brow_ps[:, 0:1023], AX.X,
                                          apply_absolute_value=True), inc=("V", 1),
         wait=[("V", k.cnt["V"] - 1)])
    relprep_V = k.cnt["V"]
    relprep_gq = k.cnt["bx"]
    sc1.close()

    def emit_ln(th):
        t0 = th * TH
        sq_prev = 0
        for dc in range(DC):
            cv = k.op("vector", lambda e, dc=dc, t0=t0:
                      e.tensor_copy(xbt[:], XT[dc][:, t0:t0 + TH]), inc=("V", 1),
                      wait=[("T", k.cnt["T"]), ("S", sq_prev)])
            sq_prev = k.op("scalar", lambda e:
                           e.activation(out=xsq[:], in_=xbt[:], func=AF.Square),
                           inc=("S", 1), wait=[("V", cv), ("T", k.cnt["T"])])
            k.op("tensor", lambda e, dc=dc:
                 e.matmul(ps_d[0:1, 0:TH], ones_col_b[:], xbt[:],
                          start=(dc == 0), stop=(dc == DC - 1)),
                 inc=("T", 1),
                 wait=[("V", cv)] + ([("V", k.cnt["V"])] if dc == 0 else []))
            k.op("tensor", lambda e, dc=dc:
                 e.matmul(ps_d[0:1, 512:512 + TH], ones_col_b[:], xsq[:],
                          start=(dc == 0), stop=(dc == DC - 1)),
                 inc=("T", 1), wait=[("S", sq_prev)])
        tS = k.cnt["T"]
        va1 = k.op("vector", lambda e: e.tensor_scalar_mul(rowA[:], ps_d[0:1, 0:TH], 1.0 / D),
                   inc=("V", 1), wait=[("T", tS)])
        vb1 = k.op("vector", lambda e: e.tensor_scalar_mul(rowB[:], ps_d[0:1, 512:512 + TH],
                                                           1.0 / D), inc=("V", 1))
        vc1 = k.op("vector", lambda e: e.tensor_mul(rowC[:], rowA[:], rowA[:]), inc=("V", 1),
                   wait=[("V", va1)])
        v4 = k.op("vector", lambda e: e.scalar_tensor_tensor(
            rowB[:], rowB[:], float(EPS), rowC[:], ALU.add, ALU.subtract), inc=("V", 1),
            wait=[("V", vc1)])
        s1 = k.op("scalar", lambda e: e.activation(out=rowB[:], in_=rowB[:], func=AF.Ln),
                  inc=("S", 1), wait=[("V", v4)])
        s2 = k.op("scalar", lambda e: e.activation(out=rowB[:], in_=rowB[:], func=AF.Exp,
                                                   scale=-0.5), inc=("S", 1),
                  wait=[("S", s1)])
        vbf = k.op("vector", lambda e: e.tensor_copy(rowBF[:], rowB[:]), inc=("V", 1),
                   wait=[("S", s2)])
        vc2 = k.op("vector", lambda e: e.tensor_mul(rowC[:], rowA[:], rowB[:]), inc=("V", 1))
        vc3 = k.op("vector", lambda e: e.tensor_scalar_mul(rowC[:], rowC[:], -1.0),
                   inc=("V", 1), wait=[("V", vc2)])
        v8 = k.op("vector", lambda e: e.tensor_copy(rowBF2[:], rowC[:]), inc=("V", 1),
                  wait=[("V", vc3)])
        tA = k.op("tensor", lambda e: e.matmul(ps_d[:, 0:TH], ones_row_b[:], rowBF[:],
                                               start=True, stop=True),
                  inc=("T", 1), wait=[("V", v8)])
        vA = k.op("vector", lambda e: e.tensor_copy(Abc[:], ps_d[:, 0:TH]), inc=("V", 1),
                  wait=[("T", tA)])
        tB = k.op("tensor", lambda e: e.matmul(ps_d[:, 512:512 + TH], ones_row_b[:],
                                               rowBF2[:], start=True, stop=True),
                  inc=("T", 1), wait=[("V", vA)])
        k.op("vector", lambda e: e.tensor_copy(Bbc[:], ps_d[:, 512:512 + TH]), inc=("V", 1),
             wait=[("T", tB)])
        for dc in range(DC):
            vt1 = k.op("vector", lambda e, dc=dc, t0=t0:
                       e.tensor_mul(tmpf[:], XT[dc][:, t0:t0 + TH], Abc[:]), inc=("V", 1))
            k.op("vector", lambda e, dc=dc:
                 e.tensor_add(LN1[dc][:, 0:TH], tmpf[:], Bbc[:]), inc=("V", 1),
                 wait=[("V", vt1)])
        return k.cnt["V"]

    def emit_ffn(first_app, wstr, Z, SG):
        silu_hist = {}
        z_hist = {}
        strip_t = {"wsA": k.cnt["T"], "wsB": k.cnt["T"]}
        ffn_entry_V = k.cnt["V"]
        for th in range(2):
            lnv = emit_ln(th)
            for hc in range(HC):
                slot = hc % 2
                ssem = "wsA" if slot == 0 else "wsB"
                r, l = hc // HCP, hc % HCP
                g_row = (r * 2 * HCP + l) * 128
                u_row = (r * 2 * HCP + HCP + l) * 128
                wcc = [("cc", cc_gu)] if (first_app and th == 0 and hc < 2) else []
                lg = k.op("sync", lambda e, g_row=g_row, slot=slot:
                          e.dma_start(out=wstr[slot][:, 0:2048],
                                      in_=pgu_all[g_row:g_row + 128, :]),
                          inc=(ssem, 16), wait=wcc + [("T", strip_t[ssem])])
                lu = k.op("sync", lambda e, u_row=u_row, slot=slot:
                          e.dma_start(out=wstr[slot][:, 2048:4096],
                                      in_=pgu_all[u_row:u_row + 128, :]),
                          inc=(ssem, 16))
                h = hc % 2
                for dc in range(DC):
                    w0 = None
                    if dc == 0:
                        w0 = [(ssem, lu), ("V", lnv),
                              ("S", silu_hist.get((th, hc - 2), 0))]
                        if th == 0 and hc < 2:
                            w0.append(("V", ffn_entry_V))
                    k.op("tensor", lambda e, slot=slot, dc=dc, h=h:
                         e.matmul(psA(h), wstr[slot][:, dc * 128:(dc + 1) * 128],
                                  LN1[dc][:, 0:TH],
                                  start=(dc == 0), stop=(dc == DC - 1)),
                         inc=("T", 1) if dc == DC - 1 else None, wait=w0)
                tg = k.cnt["T"]
                for dc in range(DC):
                    w0 = ([(ssem, lu), ("V", z_hist.get((th, hc - 2), 0))]
                          if dc == 0 else None)
                    k.op("tensor", lambda e, slot=slot, dc=dc, h=h:
                         e.matmul(psB(h),
                                  wstr[slot][:, 2048 + dc * 128: 2048 + (dc + 1) * 128],
                                  LN1[dc][:, 0:TH],
                                  start=(dc == 0), stop=(dc == DC - 1)),
                         inc=("T", 1) if dc == DC - 1 else None, wait=w0)
                tu = k.cnt["T"]
                strip_t[ssem] = tu
                sg2 = k.op("scalar", lambda e, h=h:
                           e.activation(out=SG[:], in_=psA(h), func=AF.Silu),
                           inc=("S", 1),
                           wait=[("T", tg), ("V", z_hist.get((th, hc - 1), 0))])
                silu_hist[(th, hc)] = sg2
                zv = k.op("vector", lambda e, hc=hc, h=h:
                          e.tensor_tensor(Z[:, hc * TH:(hc + 1) * TH], psB(h), SG[:],
                                          ALU.mult),
                          inc=("V", 1), wait=[("T", tu), ("S", sg2)])
                z_hist[(th, hc)] = zv
            zall = k.cnt["V"]
            for dmc in range(DC):
                slot = dmc % 2
                ssem = "wsA" if slot == 0 else "wsB"
                wcc = [("cc", cc_d)] if (first_app and th == 0 and dmc < 2) else []
                ld = k.op("sync", lambda e, dmc=dmc, slot=slot:
                          e.dma_start(out=wstr[slot][:, 0:H],
                                      in_=pd_all[dmc * 128:(dmc + 1) * 128, :]),
                          inc=(ssem, 16), wait=wcc + [("T", strip_t[ssem])])
                h = dmc % 2
                for hcc in range(HC):
                    w0 = ([(ssem, ld), ("V", zall), ("V", k.cnt["V"])]
                          if hcc == 0 else None)
                    k.op("tensor", lambda e, slot=slot, hcc=hcc, h=h:
                         e.matmul(psC(h), wstr[slot][:, hcc * 128:(hcc + 1) * 128],
                                  Z[:, hcc * TH:(hcc + 1) * TH],
                                  start=(hcc == 0), stop=(hcc == HC - 1)),
                         inc=("T", 1) if hcc == HC - 1 else None, wait=w0)
                td = k.cnt["T"]
                strip_t[ssem] = td
                k.op("vector", lambda e, dmc=dmc, h=h, th=th:
                     e.tensor_add(XT[dmc][:, th * TH:(th + 1) * TH], psC(h),
                                  XT[dmc][:, th * TH:(th + 1) * TH]),
                     inc=("V", 1), wait=[("T", td)])

    if STAGE >= 1:
        sc_f1 = contextlib.ExitStack()
        wstr1 = [sc_f1.enter_context(sb(f"w1s{i}", [128, H], BF16)) for i in range(2)]
        Z1 = sc_f1.enter_context(sb("Z1", [128, HC * TH], BF16))
        SG1 = sc_f1.enter_context(sb("SG1", [128, TH], BF16))
        emit_ffn(True, wstr1, Z1, SG1)
        sc_f1.close()

    if STAGE >= 2:
        sc2 = contextlib.ExitStack()
        idxs = sc2.enter_context(sb("idxs", [128, 192], I32))
        Mneg = sc2.enter_context(sb("Mneg", [128, 32], F32))
        Bsat = sc2.enter_context(sb("Bsat", [128, 2048], F32))
        band = sc2.enter_context(sb("band", [128, NBAND * 512], BF16))

        scD = contextlib.ExitStack()
        kstage = [scD.enter_context(sb(f"kst{i}", [128, TH], BF16)) for i in range(2)]
        vstage = [scD.enter_context(sb(f"vst{i}", [128, TH], BF16)) for i in range(2)]
        vtt = [scD.enter_context(sb(f"vtt{i}", [128, 128], BF16)) for i in range(2)]
        nstage = scD.enter_context(sb("nstage", [1, TH], F32))
        nsq = scD.enter_context(sb("nsq", [128, TH], BF16))
        wstrA = [scD.enter_context(sb(f"wsa{i}", [128, 2048], BF16)) for i in range(2)]
        vr = [scD.enter_context(sb(f"vr{i}", [128, D], BF16)) for i in range(2)]
        gkbuf = scD.enter_context(sb("gkbuf", [32, 2048], F32))
        gk2 = scD.enter_context(sb("gk2", [32, 1], F32))
        gq2 = scD.enter_context(sb("gq2", [32, 1], F32))
        gtmp = scD.enter_context(sb("gtmp", [32, 1], F32))
        mrow = scD.enter_context(sb("mrow", [1, 32], F32))
        bmax_bc = scD.enter_context(sb("bmax_bc", [32, 1], F32))
        brow_sat = scD.enter_context(sb("brow_sat", [1, 2048], F32))

        strip_t = {"wsA": k.cnt["T"], "wsB": k.cnt["T"]}
        ev_q = 0
        ev_v = 0
        vstage_t = [0, 0]
        vtt_g = [0, 0]
        kst_g = [0, 0]
        qkv_entry_V = k.cnt["V"]
        for th in range(2):
            lnv = emit_ln(th)
            for w in range(3):
                for ec in range(ECN):
                    slot = ec % 2
                    ssem = "wsA" if slot == 0 else "wsB"
                    r, l = ec // 2, ec % 2
                    row = (r * 4 + w) * 256 + l * 128
                    wcc = [("cc", cc_a)] if (th == 0 and w == 0 and ec < 2) else []
                    lw = k.op("sync", lambda e, row=row, slot=slot:
                              e.dma_start(out=wstrA[slot][:], in_=pa_all[row:row + 128, :]),
                              inc=(ssem, 16), wait=wcc + [("T", strip_t[ssem])])
                    h = ec % 2
                    for dc in range(DC):
                        w0 = None
                        if dc == 0:
                            w0 = [(ssem, lw), ("V", lnv), ("S", ev_q), ("V", ev_v)]
                            if th == 0 and w == 0 and ec < 2:
                                w0.append(("V", qkv_entry_V))
                        k.op("tensor", lambda e, slot=slot, dc=dc, h=h:
                             e.matmul(psA(h), wstrA[slot][:, dc * 128:(dc + 1) * 128],
                                      LN1[dc][:, 0:TH],
                                      start=(dc == 0), stop=(dc == DC - 1)),
                             inc=("T", 1) if dc == DC - 1 else None, wait=w0)
                    tq = k.cnt["T"]
                    if w < 2:
                        qk_dst = q_dram if w == 0 else kt_in
                        nrm_dst = qn_dram if w == 0 else kn_in
                        kss = "ksA" if slot == 0 else "ksB"
                        ev_q = k.op("scalar", lambda e, slot=slot, h=h, kss=kss:
                                    e.activation(out=kstage[slot][:], in_=psA(h),
                                                 func=AF.Copy),
                                    inc=("S", 1), wait=[("T", tq), (kss, kst_g[slot])])
                        kst_g[slot] = k.op(
                            "gpsimd", lambda e, ec=ec, th=th, slot=slot, qk_dst=qk_dst:
                            e.dma_start(out=qk_dst[ec * 128:(ec + 1) * 128,
                                                   th * TH:(th + 1) * TH],
                                        in_=kstage[slot][:]),
                            inc=(kss, 16), wait=[("S", ev_q)])
                        sq2 = k.op("scalar", lambda e, slot=slot:
                                   e.activation(out=nsq[:], in_=kstage[slot][:],
                                                func=AF.Square),
                                   inc=("S", 1), wait=[("T", k.cnt["T"]), ("S", ev_q)])
                    if w < 2:
                        for hh in range(2):
                            mask = mask_top if hh == 0 else mask_bot
                            head = ec * 2 + hh
                            tn = k.op("tensor", lambda e, mask=mask:
                                      e.matmul(ps_d[0:1, 0:TH], mask[:], nsq[:],
                                               start=True, stop=True),
                                      inc=("T", 1), wait=[("S", sq2), ("V", k.cnt["V"])])
                            vn = k.op("vector", lambda e:
                                      e.tensor_copy(nstage[:], ps_d[0:1, 0:TH]),
                                      inc=("V", 1), wait=[("T", tn), ("nst", k.cnt["nst"])])
                            k.op("gpsimd", lambda e, head=head, th=th, nd=nrm_dst:
                                 e.dma_start(out=nd[head:head + 1, th * TH:(th + 1) * TH],
                                             in_=nstage[:]),
                                 inc=("nst", 16), wait=[("V", vn)])
                    if w == 2:
                        ev_v = k.op("vector", lambda e, slot=slot, h=h:
                                    e.tensor_copy(vstage[slot][:], psA(h)),
                                    inc=("V", 1), wait=[("T", tq), ("T", vstage_t[slot])])
                        for tt in range(4):
                            h2 = tt % 2
                            tv = k.op("tensor", lambda e, slot=slot, tt=tt, h2=h2:
                                      e.transpose(psC(h2, 64).bitcast(BF16),
                                                  vstage[slot][:, tt * 128:(tt + 1) * 128],
                                                  idb[:]),
                                      inc=("T", 1), wait=[("V", ev_v), ("V", k.cnt["V"])])
                            vstage_t[slot] = tv
                            ti = tt % 2
                            vts = "vtA" if ti == 0 else "vtB"
                            vt = k.op("vector", lambda e, ti=ti, h2=h2, vts=vts:
                                      e.tensor_copy(vtt[ti][:], psC(h2, 64).bitcast(BF16)),
                                      inc=("V", 1), wait=[("T", tv), (vts, vtt_g[ti])])
                            vtt_g[ti] = k.op(
                                "gpsimd",
                                lambda e, ec=ec, th=th, tt=tt, ti=ti:
                                e.dma_start(out=v_in[th * TH + tt * 128:
                                                     th * TH + (tt + 1) * 128,
                                                     ec * 128:(ec + 1) * 128],
                                            in_=vtt[ti][:]),
                                inc=(vts, 16), wait=[("V", vt)])
                    strip_t[ssem] = k.cnt["T"]
        qkv_bar = [("ksA", k.cnt["ksA"]), ("ksB", k.cnt["ksB"]),
                   ("vtA", k.cnt["vtA"]), ("vtB", k.cnt["vtB"]),
                   ("nst", k.cnt["nst"])]
        cc_kt = k.op("gpsimd", lambda e: e.collective_compute(
            "AllGather", ALU.bypass, replica_groups=[list(range(NC))],
            ins=[kt_in.ap().opt()], outs=[kt_out.ap().opt()]),
            inc=("cc", 1), wait=qkv_bar)
        cc_v = k.op("gpsimd", lambda e: e.collective_compute(
            "AllGather", ALU.bypass, replica_groups=[list(range(NC))],
            ins=[v_in.ap().opt()], outs=[v_out.ap().opt()]),
            inc=("cc", 1))
        cc_kn = k.op("gpsimd", lambda e: e.collective_compute(
            "AllGather", ALU.bypass, replica_groups=[list(range(NC))],
            ins=[kn_in.ap().opt()], outs=[kn_out.ap().opt()]),
            inc=("cc", 1))

        ld_idx = k.op("sync", lambda e: e.dma_start(out=idxs[:], in_=idx_tab[:, :]),
                      inc=("idx", 16))
        vr_store = [0, 0]
        for kc in range(KCN):
            slot = kc % 2
            vrs = "vrA" if slot == 0 else "vrB"
            gv = k.op("gpsimd", lambda e, kc=kc, slot=slot:
                      e.indirect_dma_start(
                          vr[slot][:, :], None, v_out[:, :],
                          IndirectOffsetOnAxis(idxs[:, 128 + kc:129 + kc], 0)),
                      inc=(vrs, 16),
                      wait=([("cc", cc_v), ("idx", ld_idx)] if kc == 0
                            else [(vrs, vr_store[slot])]))
            vr_store[slot] = k.op(
                "gpsimd", lambda e, kc=kc, slot=slot:
                e.dma_start(out=v_rot[kc * 128:(kc + 1) * 128, :], in_=vr[slot][:]),
                inc=(vrs, 16), wait=[(vrs, gv)])
        vrot_bar = [("vrA", k.cnt["vrA"]), ("vrB", k.cnt["vrB"])]

        vk = k.op("vector", lambda e: e.memset(gk2[:], 0.0), inc=("V", 1))
        for part in range(4):
            lkn = k.op("sync", lambda e, part=part: e.dma_start(
                out=gkbuf[:].rearrange("h (r t) -> h r t", r=2),
                in_=dview(kn_out, part * 2 * HEADS * T,
                          [[T, 32], [HEADS * T, 2], [1, T]])),
                inc=("gkb", 16), wait=[("cc", cc_kn), ("V", vk)])
            vred = k.op("vector", lambda e: e.reduce_max(gtmp[:], gkbuf[:], AX.X),
                        inc=("V", 1), wait=[("gkb", lkn)])
            vk = k.op("vector", lambda e: e.tensor_max(gk2[:], gk2[:], gtmp[:]),
                      inc=("V", 1), wait=[("V", vred)])
        lqn = k.op("sync", lambda e: e.dma_start(
            out=gkbuf[:, 0:T], in_=qn_dram[:, :]),
            inc=("gkb", 16), wait=[("V", vk)] + qkv_bar)
        vq = k.op("vector", lambda e: e.reduce_max(gq2[:], gkbuf[:, 0:T], AX.X),
                  inc=("V", 1), wait=[("gkb", lqn)])
        tb = k.op("tensor", lambda e: e.matmul(ps_d[0:32, 0:1], ones_row_f[:, 0:32],
                                               bmax_sb[:], start=True, stop=True),
                  inc=("T", 1), wait=[("V", relprep_V), ("V", k.cnt["V"])])
        vbm = k.op("vector", lambda e: e.tensor_copy(bmax_bc[:], ps_d[0:32, 0:1]),
                   inc=("V", 1), wait=[("T", tb)])
        vs1 = k.op("vector", lambda e: e.tensor_add(gtmp[:], gq2[:], gk2[:]), inc=("V", 1),
                   wait=[("V", vq), ("V", vbm)])
        vM = k.op("vector", lambda e: e.scalar_tensor_tensor(
            gq2[:], gtmp[:], -1.0 / 16, bmax_bc[:], ALU.mult, ALU.subtract),
            inc=("V", 1), wait=[("V", vs1)])
        tM = k.op("tensor", lambda e: e.transpose(ps_d[0:1, 0:32], gq2[:],
                                                  idf[0:32, 0:32]),
                  inc=("T", 1), wait=[("V", vM)])
        vMr = k.op("vector", lambda e: e.tensor_copy(mrow[:], ps_d[0:1, 0:32]),
                   inc=("V", 1), wait=[("T", tM)])
        tMn = k.op("tensor", lambda e: e.matmul(ps_d[:, 0:32], ones_row_f[:], mrow[:],
                                                start=True, stop=True),
                   inc=("T", 1), wait=[("V", vMr)])
        k.op("vector", lambda e: e.tensor_copy(Mneg[:], ps_d[:, 0:32]),
             inc=("V", 1), wait=[("T", tMn)])
        for hh in range(HEADS):
            k.op("vector", lambda e, hh=hh: e.tensor_scalar_add(
                fview(brow_sat, hh * 64, [[32, 2], [1, 32]]),
                fview(brow_ps, NP_P, [[64, 2], [2, 32]]),
                mrow[0:1, hh:hh + 1]), inc=("V", 1),
                wait=[("V", vMr)] if hh == 0 else None)
        for n0 in range(0, 2048, 512):
            h = (n0 // 512) % 2
            tB2 = k.op("tensor", lambda e, n0=n0, h=h:
                       e.matmul(psD(h), ones_row_f[:], brow_sat[0:1, n0:n0 + 512],
                                start=True, stop=True),
                       inc=("T", 1), wait=[("V", k.cnt["V"])])
            k.op("vector", lambda e, n0=n0, h=h:
                 e.tensor_copy(Bsat[:, n0:n0 + 512], psD(h)),
                 inc=("V", 1), wait=[("T", tB2)])

        if DEBUG:
            k.op("gpsimd", lambda e: e.dma_start(out=dbg_m.ap().ap and dbg_m[0:128, 0:32], in_=Mneg[:]),
                 inc=("oqA", 16), wait=[("V", k.cnt["V"])])
            k.op("gpsimd", lambda e: e.dma_start(out=dbg_m[0:32, 32:33], in_=gk2[:], allow_slow_non_contiguous=True),
                 inc=("oqA", 16))
            k.op("gpsimd", lambda e: e.dma_start(out=dbg_m[0:32, 33:34], in_=gq2[:], allow_slow_non_contiguous=True),
                 inc=("oqA", 16))
            k.op("gpsimd", lambda e: e.dma_start(out=dbg_m[0:1, 34:35], in_=bmax_sb[:]),
                 inc=("oqA", 16))
            k.op("gpsimd", lambda e: e.dma_start(out=dbg_m[0:128, 35:67], in_=Bsat[:, 0:32]),
                 inc=("oqA", 16))
        for p in range(128):
            k.op("sync", lambda e, p=p: e.dma_start(
                out=band[p:p + 1, :].rearrange("o (t j) -> o t j", t=NBAND),
                in_=dview(bexp_dram, 127 - p, [[0, 1], [640, NBAND], [1, 512]])),
                inc=("sq", 16), wait=[("bx", relprep_gq)] if p == 0 else None)
        band_sq = k.cnt["sq"]
        bias_V = k.cnt["V"]
        scD.close()

        scE = contextlib.ExitStack()
        KP = scE.enter_context(sb("KP", [128, S], BF16))
        QTs = scE.enter_context(sb("QTs", [128, T], BF16))
        VP = scE.enter_context(sb("VP", [128, KCN * 130], BF16))
        EA = scE.enter_context(sb("EA", [128, 1024], BF16))
        EB = scE.enter_context(sb("EB", [128, 16 * TH], BF16))
        ctxacc = scE.enter_context(sb("ctxacc", [128, 520], F32))
        rd8 = scE.enter_context(sb("rd8", [128, 8], F32))
        ctxn = [scE.enter_context(sb(f"ctxn{i}", [128, 128], BF16)) for i in range(2)]

        vone = k.op("vector", lambda e: e.memset(VP[:], 1.0), inc=("V", 1),
                    wait=[("T", k.cnt["T"])] + vrot_bar + [("cc", cc_kt)])

        pvb = [(ps_c, 0), (ps_c, 512), (ps_d, 0), (ps_d, 512)]
        ctxacc_p = list(ctxacc[:].ap[0])

        def acc_view(off):
            return bass_rust.AP(ctxacc[:].tensor, ctxacc[:].offset + off,
                                [ctxacc_p, [65, 2], [1, 65]])

        def pv_view(t):
            a = t[:]
            return bass_rust.AP(a.tensor, a.offset, [list(a.ap[0]), [512, 2], [1, 65]])

        seA_h = {}
        pvA_h = {}
        last_pvB = 0
        last_foldB = 0
        ctx_dma = [0, 0]
        for pair in range(PAIRS):
            kp_t = 0
            for u in range(8):
                kp_t = k.op("gpsimd", lambda e, u=u, pair=pair:
                            e.indirect_dma_start(
                                KP[:, u * 1024:(u + 1) * 1024], None, kt_out[:, :],
                                IndirectOffsetOnAxis(
                                    idxs[:, u * 16 + pair:u * 16 + pair + 1], 0)),
                            inc=("kp", 16),
                            wait=([("cc", cc_kt), ("idx", ld_idx)] + vrot_bar
                                  if pair == 0 and u == 0 else
                                  ([("T", k.cnt["T"])] if u == 0 else None)))
            vp_t = k.op("sync", lambda e, pair=pair: e.dma_start(
                out=fview(VP, 0, [[130, KCN], [1, 64]]),
                in_=dview(v_rot, 128 * pair,
                          [[D, 128], [128 * D, KCN], [1, 64]])),
                inc=("vp", 16),
                wait=(vrot_bar + [("V", vone), ("sq", band_sq)] if pair == 0
                      else [("T", k.cnt["T"])]))
            vp_t = k.op("sync", lambda e, pair=pair: e.dma_start(
                out=fview(VP, 65, [[130, KCN], [1, 64]]),
                in_=dview(v_rot, 128 * pair + 64,
                          [[D, 128], [128 * D, KCN], [1, 64]])),
                inc=("vp", 16))
            vp_t = k.op("sync", lambda e, pair=pair: e.dma_start(
                out=QTs[:], in_=q_dram[pair * 128:(pair + 1) * 128, :]),
                inc=("vp", 16), wait=(qkv_bar + vrot_bar) if pair == 0 else None)
            if DEBUG and pair == 0:
                k.op("gpsimd", lambda e: e.dma_start(out=dbg_vp.ap(), in_=VP[:]),
                     inc=("oqA", 16), wait=[("vp", vp_t)])
            for qt in range(2):
                k.op("vector", lambda e: e.memset(ctxacc[:], 0.0), inc=("V", 1),
                     wait=[("oqA", k.cnt["oqA"])] if DEBUG else None)
                vz = k.cnt["V"]
                for sc in range(4):
                    pvA_last = 0
                    eb_free = last_pvB
                    for ki in range(16):
                        kc = sc * 16 + ki
                        h = kc % 2
                        hA, hB = 2 * pair, 2 * pair + 1
                        is_mult = (qt, kc) in MIDX
                        m = MIDX.get((qt, kc))
                        kpr = kc // 2
                        w0 = [("S", seA_h.get(kc - 2, 0))]
                        if kc == 0 and qt == 0:
                            w0 += [("kp", kp_t), ("vp", vp_t), ("V", bias_V)]
                            if pair == 0:
                                w0 += [("S", k.cnt["S"])]
                        tsA = k.op("tensor", lambda e, kc=kc, h=h, pair=pair, qt=qt:
                                   e.matmul(psA(h), KP[0:64, kc * 128:(kc + 1) * 128],
                                            QTs[0:64, qt * TH:(qt + 1) * TH],
                                            start=True, stop=True, tile_position=(0, 0)),
                                   inc=("T", 1), wait=w0)
                        tsB = k.op("tensor", lambda e, kc=kc, h=h, pair=pair, qt=qt:
                                   e.matmul(psB(h), KP[64:128, kc * 128:(kc + 1) * 128],
                                            QTs[64:128, qt * TH:(qt + 1) * TH],
                                            start=True, stop=True, tile_position=(64, 0)),
                                   inc=("T", 1))
                        if is_mult:
                            biasA = Mneg[:, hA:hA + 1]
                            biasB = Mneg[:, hB:hB + 1]
                        else:
                            fA = hA * 64 + qt * 32 + kpr
                            fB = hB * 64 + qt * 32 + kpr
                            biasA = Bsat[:, fA:fA + 1]
                            biasB = Bsat[:, fB:fB + 1]
                        seA = k.op("scalar", lambda e, h=h, biasA=biasA:
                                   e.activation(out=EA[:, h * 512: h * 512 + TH],
                                                in_=psA(h), func=AF.Exp,
                                                bias=biasA, scale=0.125),
                                   inc=("S", 1),
                                   wait=[("T", tsA), ("T", pvA_h.get(kc - 2, 0))])
                        seA_h[kc] = seA
                        seB = k.op("scalar", lambda e, ki=ki, h=h, biasB=biasB:
                                   e.activation(out=EB[:, ki * 512:(ki + 1) * 512],
                                                in_=psB(h), func=AF.Exp,
                                                bias=biasB, scale=0.125),
                                   inc=("S", 1), wait=[("T", tsB), ("T", eb_free)])
                        wA = [("S", seA)]
                        if is_mult:
                            bvA = k.op("vector", lambda e, h=h, m=m:
                                       e.tensor_tensor(EA[:, h * 512: h * 512 + TH],
                                                       EA[:, h * 512: h * 512 + TH],
                                                       band[:, m * 512:(m + 1) * 512],
                                                       ALU.mult),
                                       inc=("V", 1), wait=[("S", seA)])
                            k.op("vector", lambda e, ki=ki, m=m:
                                 e.tensor_tensor(EB[:, ki * 512:(ki + 1) * 512],
                                                 EB[:, ki * 512:(ki + 1) * 512],
                                                 band[:, m * 512:(m + 1) * 512],
                                                 ALU.mult),
                                 inc=("V", 1), wait=[("S", seB)])
                            wA = [("V", bvA)]
                        for s4 in range(4):
                            t_, off = pvb[s4]
                            wpv = None
                            if s4 == 0:
                                wpv = wA + ([("V", last_foldB)] if ki == 0 else [])
                            pvA_last = k.op(
                                "tensor",
                                lambda e, s4=s4, t_=t_, off=off, h=h, kc=kc, ki=ki:
                                e.matmul(t_[:, off:off + 65],
                                         EA[:, h * 512 + s4 * 128: h * 512 + (s4 + 1) * 128],
                                         VP[:, kc * 130: kc * 130 + 65],
                                         start=(ki == 0), stop=(ki == 15)),
                                inc=("T", 1) if s4 == 3 else None, wait=wpv)
                        pvA_h[kc] = pvA_last
                    fA1 = k.op("vector", lambda e: e.tensor_tensor(
                        acc_view(0), pv_view(ps_c), acc_view(0), ALU.add),
                        inc=("V", 1), wait=[("T", pvA_last), ("V", vz)])
                    fA2 = k.op("vector", lambda e: e.tensor_tensor(
                        acc_view(130), pv_view(ps_d), acc_view(130), ALU.add),
                        inc=("V", 1))
                    pvB_last = 0
                    sB_done = k.cnt["S"]
                    vB_done = k.cnt["V"]
                    for ki in range(16):
                        kc = sc * 16 + ki
                        for s4 in range(4):
                            t_, off = pvb[s4]
                            wpv = ([("V", fA2), ("S", sB_done), ("V", vB_done)]
                                   if ki == 0 and s4 == 0 else None)
                            pvB_last = k.op(
                                "tensor",
                                lambda e, s4=s4, t_=t_, off=off, kc=kc, ki=ki:
                                e.matmul(t_[:, off:off + 65],
                                         EB[:, ki * 512 + s4 * 128: ki * 512 + (s4 + 1) * 128],
                                         VP[:, kc * 130 + 65: kc * 130 + 130],
                                         start=(ki == 0), stop=(ki == 15)),
                                inc=("T", 1) if (ki == 15 and s4 == 3) else None,
                                wait=wpv)
                    last_pvB = pvB_last
                    k.op("vector", lambda e: e.tensor_tensor(
                        acc_view(260), pv_view(ps_c), acc_view(260), ALU.add),
                        inc=("V", 1), wait=[("T", pvB_last)])
                    last_foldB = k.op("vector", lambda e: e.tensor_tensor(
                        acc_view(390), pv_view(ps_d), acc_view(390), ALU.add),
                        inc=("V", 1))
                if DEBUG and pair == 0 and qt == 0:
                    k.op("gpsimd", lambda e: e.dma_start(out=dbg_acc.ap(), in_=ctxacc[:]),
                         inc=("oqA", 16), wait=[("V", k.cnt["V"])])
                rc1 = k.op("vector", lambda e: e.reciprocal(
                    fview(rd8, 0, [[1, 4], [1, 1]]),
                    bass_rust.AP(ctxacc[:].tensor, ctxacc[:].offset + 64,
                                 [ctxacc_p, [65, 4], [1, 1]])),
                    inc=("V", 1), wait=[("V", last_foldB)])
                rc2 = k.op("vector", lambda e: e.reciprocal(
                    fview(rd8, 4, [[1, 4], [1, 1]]),
                    bass_rust.AP(ctxacc[:].tensor, ctxacc[:].offset + 324,
                                 [ctxacc_p, [65, 4], [1, 1]])),
                    inc=("V", 1))
                for s4 in range(4):
                    ci = s4 % 2
                    cxs = "cxA" if ci == 0 else "cxB"
                    k.op("vector", lambda e, s4=s4, ci=ci, cxs=cxs: e.tensor_scalar(
                        ctxn[ci][:, 0:64], ctxacc[:, s4 * 65: s4 * 65 + 64],
                        rd8[:, s4:s4 + 1], None, ALU.mult),
                        inc=("V", 1), wait=[(cxs, ctx_dma[ci]), ("V", rc2)])
                    vb2 = k.op("vector", lambda e, s4=s4, ci=ci: e.tensor_scalar(
                        ctxn[ci][:, 64:128], ctxacc[:, 260 + s4 * 65: 260 + s4 * 65 + 64],
                        rd8[:, 4 + s4:5 + s4], None, ALU.mult),
                        inc=("V", 1))
                    ctx_dma[ci] = k.op(
                        "gpsimd", lambda e, s4=s4, ci=ci, pair=pair, qt=qt, cxs=cxs:
                        e.dma_start(out=ctx_dram[qt * TH + s4 * 128: qt * TH + (s4 + 1) * 128,
                                                 pair * 128:(pair + 1) * 128],
                                    in_=ctxn[ci][:]),
                        inc=(cxs, 16), wait=[("V", vb2)])
        attn_bar = [("cxA", k.cnt["cxA"]), ("cxB", k.cnt["cxB"])]
        scE.close()

        scF = contextlib.ExitStack()
        ctile = [scF.enter_context(sb(f"ctile{i}", [128, D], BF16)) for i in range(2)]
        ctxT = [scF.enter_context(sb(f"ctxT{i}", [128, T], BF16)) for i in range(DC)]
        wstrF = [scF.enter_context(sb(f"wsf{i}", [128, 2048], BF16)) for i in range(2)]
        ct_t = [k.cnt["T"], k.cnt["T"]]
        f_entry_V = k.cnt["V"]
        for tt in range(8):
            slot = tt % 2
            cts = "ctA" if slot == 0 else "ctB"
            lc = k.op("sync", lambda e, tt=tt, slot=slot:
                      e.dma_start(out=ctile[slot][:],
                                  in_=ctx_dram[tt * 128:(tt + 1) * 128, :]),
                      inc=(cts, 16),
                      wait=attn_bar + [("T", ct_t[slot])])
            for dc in range(DC):
                h2 = dc % 2
                tv = k.op("tensor", lambda e, slot=slot, dc=dc, h2=h2:
                          e.transpose(psC(h2, 64).bitcast(BF16),
                                      ctile[slot][:, dc * 128:(dc + 1) * 128], idb[:]),
                          inc=("T", 1),
                          wait=[(cts, lc), ("V", k.cnt["V"])]
                               + ([("V", f_entry_V)] if tt == 0 and dc == 0 else []))
                k.op("vector", lambda e, dc=dc, tt=tt, h2=h2:
                     e.tensor_copy(ctxT[dc][:, tt * 128:(tt + 1) * 128],
                                   psC(h2, 64).bitcast(BF16)),
                     inc=("V", 1), wait=[("T", tv)])
            ct_t[slot] = k.cnt["T"]
        ctxT_V = k.cnt["V"]
        strip_t = {"wsA": ct_t[0], "wsB": ct_t[1]}
        for fc in range(ECN):
            slot = fc % 2
            ssem = "wsA" if slot == 0 else "wsB"
            r, l = fc // 2, fc % 2
            row = (r * 4 + 3) * 256 + l * 128
            lw = k.op("sync", lambda e, row=row, slot=slot:
                      e.dma_start(out=wstrF[slot][:], in_=pa_all[row:row + 128, :]),
                      inc=(ssem, 16),
                      wait=[("T", strip_t[ssem])] + (attn_bar if fc < 2 else []))
            for th in range(2):
                h = (fc * 2 + th) % 2
                for ec in range(DC):
                    w0 = ([(ssem, lw), ("V", ctxT_V), ("V", k.cnt["V"])]
                          if ec == 0 else None)
                    k.op("tensor", lambda e, slot=slot, ec=ec, h=h, th=th:
                         e.matmul(psA(h), wstrF[slot][:, ec * 128:(ec + 1) * 128],
                                  ctxT[ec][:, th * TH:(th + 1) * TH],
                                  start=(ec == 0), stop=(ec == DC - 1)),
                         inc=("T", 1) if ec == DC - 1 else None, wait=w0)
                to = k.cnt["T"]
                k.op("vector", lambda e, fc=fc, th=th, h=h:
                     e.tensor_add(XT[fc][:, th * TH:(th + 1) * TH], psA(h),
                                  XT[fc][:, th * TH:(th + 1) * TH]),
                     inc=("V", 1), wait=[("T", to)])
            strip_t[ssem] = k.cnt["T"]
        scF.close()
        sc2.close()

    if STAGE >= 3:
        sc_f2 = contextlib.ExitStack()
        wstr2 = [sc_f2.enter_context(sb(f"w2s{i}", [128, H], BF16)) for i in range(2)]
        Z2 = sc_f2.enter_context(sb("Z2", [128, HC * TH], BF16))
        SG2 = sc_f2.enter_context(sb("SG2", [128, TH], BF16))
        emit_ffn(False, wstr2, Z2, SG2)
        sc_f2.close()

    ostage = [stack.enter_context(sb(f"os{i}", [128, D], F32)) for i in range(2)]
    out_dma = [0, 0]
    for tt in range(8):
        slot = tt % 2
        oqs = "oqA" if slot == 0 else "oqB"
        for dc in range(DC):
            h2 = dc % 2
            tv = k.op("tensor", lambda e, tt=tt, dc=dc, h2=h2:
                      e.transpose(psC(h2, 128), XT[dc][:, tt * 128:(tt + 1) * 128],
                                  idf[:]),
                      inc=("T", 1),
                      wait=[("V", k.cnt["V"]), (oqs, out_dma[slot])])
            k.op("vector", lambda e, slot=slot, dc=dc, h2=h2:
                 e.tensor_copy(ostage[slot][:, dc * 128:(dc + 1) * 128], psC(h2, 128)),
                 inc=("V", 1), wait=[("T", tv)])
        sv = k.cnt["V"]
        out_dma[slot] = k.op("gpsimd", lambda e, tt=tt, slot=slot:
                             e.dma_start(out=y_out[tt * 128:(tt + 1) * 128, :],
                                         in_=ostage[slot][:]),
                             inc=(oqs, 16), wait=[("V", sv)])
    if DEBUG and STAGE >= 2:
        for (dst, srct) in ((dbg_ctx, ctx_dram), (dbg_q, q_dram), (dbg_kt, kt_out),
                            (dbg_vr, v_rot), (dbg_kn, kn_out), (dbg_vin, v_in)):
            k.op("gpsimd", lambda e, dst=dst, srct=srct:
                 e.dma_start(out=dst.ap(), in_=srct.ap()), inc=("oqA", 16))
        k.op("gpsimd", lambda e:
             e.dma_start(out=dbg_bx.ap(), in_=bexp_dram.ap()), inc=("oqA", 16))
    k.wait("gpsimd", "oqA", k.cnt["oqA"])
    k.wait("gpsimd", "oqB", k.cnt["oqB"])

    with nc.Block() as block:
        @block.sync
        def _(e):
            for f in k.st["sync"]:
                f(e)

        @block.gpsimd
        def _(e):
            for f in k.st["gpsimd"]:
                f(e)

        @block.vector
        def _(e):
            for f in k.st["vector"]:
                f(e)

        @block.scalar
        def _(e):
            for f in k.st["scalar"]:
                f(e)

        @block.tensor
        def _(e):
            for f in k.st["tensor"]:
                f(e)

    stack.close()
    return nc


def kernel(L0_attn_hidden_states, L0_attn_wq, L0_attn_wk, L0_attn_wv,
           L0_attn_wo, L0_attn_rel_pos, L0_ffn_wq_gate, L0_ffn_s_gate,
           L0_ffn_wq_up, L0_ffn_s_up, L0_ffn_wq_down, L0_ffn_s_down):
    x = np.ascontiguousarray(np.asarray(L0_attn_hidden_states, dtype=np.float32))
    wq = np.asarray(L0_attn_wq, np.float32)
    wk = np.asarray(L0_attn_wk, np.float32)
    wv = np.asarray(L0_attn_wv, np.float32)
    wo = np.asarray(L0_attn_wo, np.float32)
    rel = np.asarray(L0_attn_rel_pos, np.float32)
    wg = np.asarray(L0_ffn_wq_gate, np.int32)
    sg = np.asarray(L0_ffn_s_gate, np.float32)
    wu = np.asarray(L0_ffn_wq_up, np.int32)
    su = np.asarray(L0_ffn_s_up, np.float32)
    wd = np.asarray(L0_ffn_wq_down, np.int32)
    sd = np.asarray(L0_ffn_s_down, np.float32)

    identf = np.eye(128, dtype=np.float32)
    identb = np.eye(128, dtype=ml_dtypes.bfloat16)

    def pad_rows(a, n):
        if a.shape[0] == n:
            return a
        out = np.zeros((n,) + a.shape[1:], a.dtype)
        out[:a.shape[0]] = a
        return out

    in_maps = []
    for c in range(NC):
        g0, g1 = 768 * c, min(768 * (c + 1), H)
        in_maps.append({
            "x_in": np.ascontiguousarray(x[0, T * c:T * (c + 1), :]),
            "wg_sl": pad_rows(np.ascontiguousarray(wg[g0:g1]), 768),
            "wu_sl": pad_rows(np.ascontiguousarray(wu[g0:g1]), 768),
            "wd_sl": np.ascontiguousarray(wd[256 * c:256 * (c + 1)]),
            "sg_sl": np.ascontiguousarray(pad_rows(sg[g0:g1], 768).reshape(HCP, 128).T),
            "su_sl": np.ascontiguousarray(pad_rows(su[g0:g1], 768).reshape(HCP, 128).T),
            "sd_sl": np.ascontiguousarray(sd[256 * c:256 * (c + 1)].reshape(2, 128).T),
            "wa_sl": np.ascontiguousarray(np.concatenate(
                [w[256 * c:256 * (c + 1)] for w in (wq, wk, wv, wo)], axis=0)),
            "rel_tabt": _host_rel_table(rel, c).astype(np.float32),
            "idx_tab": _host_idx_tables(c),
            "ident_f": identf,
            "ident_b": identb,
        })

    nc = build_nc()
    res = run_bass_kernel_spmd(nc, in_maps, core_ids=list(range(NC)))
    if DEBUG:
        DEBUG_OUT.clear()
        for key in res.results[0]:
            DEBUG_OUT[key] = [np.asarray(res.results[c][key]) for c in range(NC)]
    y = np.concatenate([res.results[c]["y"] for c in range(NC)], axis=0)
    return y[None, :, :].astype(np.float32)


# version 03
# speedup vs baseline: 1.1962x; 1.1962x over previous
import contextlib
import numpy as np
import ml_dtypes
import bass_rust
import concourse.bass as bass
import concourse.mybir as mybir
from concourse.bass import IndirectOffsetOnAxis
from concourse.bass_utils import run_bass_kernel_spmd

F32 = mybir.dt.float32
BF16 = mybir.dt.bfloat16
I32 = mybir.dt.int32
AF = mybir.ActivationFunctionType
ALU = mybir.AluOpType
AX = mybir.AxisListType

NC = 8
T = 1024
D = 2048
H = 5632
S = 8192
DC = 16
HC = 44
KCN = 64
ECN = 16
HEADS = 32
PAIRS = 16
TH = 512
HCP = 6
EPS = 1e-5
HW = 2816
HWC = 22

STAGE = 3
DEBUG = False
DEBUG_OUT = {}


def _dprime(c, qt, kc_rel):
    kc_abs = (kc_rel + 8 * c) % 64
    return 128 * kc_abs - (1024 * c + 512 * qt) + 511


def _geometry():
    mult = []
    for qt in range(2):
        for kc in range(64):
            if any(-127 < _dprime(c, qt, kc) < 1533 for c in range(8)):
                mult.append((qt, kc))
    return mult, {t: i for i, t in enumerate(mult)}


MULT, MIDX = _geometry()
NBAND = len(MULT)
NP_P = 1024
NP_S = 128
NTAB = NP_P + NP_S + NBAND * 640
BAND0 = NP_P + NP_S


def _host_rel_table(rel_pos, c):
    idx = np.zeros(NTAB, dtype=np.int64)
    idx[:NP_P] = np.minimum(np.arange(NP_P), 1022)
    for qt in range(2):
        for kc in range(64):
            d = _dprime(c, qt, kc)
            idx[NP_P + qt * 64 + kc] = 0 if d <= -127 else 1022
    for m, (qt, kc) in enumerate(MULT):
        d = _dprime(c, qt, kc)
        w = np.arange(640)
        idx[BAND0 + 640 * m: BAND0 + 640 * (m + 1)] = np.clip(d + 127 - w, 0, 1022)
    return np.ascontiguousarray(rel_pos[idx].T)


def _host_idx_tables(c):
    p = np.arange(128)
    idx_k = np.zeros((128, 128), np.int32)
    for u in range(8):
        rot = (u + c) % 8
        for j in range(PAIRS):
            idx_k[:, u * 16 + j] = rot * 2048 + 128 * j + p
    idx_v = np.zeros((128, 64), np.int32)
    for kc in range(64):
        kc_abs = (kc + 8 * c) % 64
        idx_v[:, kc] = kc_abs * 128 + p
    return np.concatenate([idx_k, idx_v], axis=1)


def fview(t, off, dims):
    a = t[:]
    return bass_rust.AP(a.tensor, a.offset + off, [list(a.ap[0])] + dims)


def dview(t, off, dims):
    a = t.ap()
    return bass_rust.AP(a.tensor, off, dims)


class KB:
    ENGS = ("sync", "gpsimd", "vector", "scalar", "tensor")

    def __init__(self, nc):
        self.nc = nc
        self.st = {e: [] for e in self.ENGS}
        self.sem = {}
        self.cnt = {}

    def new_sem(self, stack, name):
        self.sem[name] = stack.enter_context(self.nc.semaphore(name))
        self.cnt[name] = 0

    def op(self, eng, f, inc=None, wait=None):
        tgt = None
        if inc is not None:
            self.cnt[inc[0]] += inc[1]
            tgt = self.cnt[inc[0]]
        waits = [(n, t) for (n, t) in (wait or []) if t > 0]
        sems = self.sem

        def g(e, f=f, waits=waits, inc=inc):
            for wn, wt in waits:
                e.wait_ge(sems[wn], wt)
            ins = f(e)
            if inc is not None and ins is not None:
                ins.then_inc(sems[inc[0]], inc[1])
        self.st[eng].append(g)
        return tgt

    def wait(self, eng, name, tgt):
        sems = self.sem
        if tgt > 0:
            self.st[eng].append(lambda e, n=name, t=tgt: e.wait_ge(sems[n], t))


def build_nc():
    nc = bass.Bass("TRN2", target_bir_lowering=False)
    stack = contextlib.ExitStack()

    x_in = nc.dram_tensor("x_in", [T, D], F32, kind="ExternalInput")
    wg_sl = nc.dram_tensor("wg_sl", [HCP * 128, D], I32, kind="ExternalInput")
    wu_sl = nc.dram_tensor("wu_sl", [HCP * 128, D], I32, kind="ExternalInput")
    wd_sl = nc.dram_tensor("wd_sl", [256, H], I32, kind="ExternalInput")
    sg_sl = nc.dram_tensor("sg_sl", [128, HCP], F32, kind="ExternalInput")
    su_sl = nc.dram_tensor("su_sl", [128, HCP], F32, kind="ExternalInput")
    sd_sl = nc.dram_tensor("sd_sl", [128, 2], F32, kind="ExternalInput")
    wa_sl = nc.dram_tensor("wa_sl", [1024, D], F32, kind="ExternalInput")
    rel_tabt = nc.dram_tensor("rel_tabt", [64, NTAB], F32, kind="ExternalInput")
    idx_tab = nc.dram_tensor("idx_tab", [128, 192], I32, kind="ExternalInput")
    ident_f = nc.dram_tensor("ident_f", [128, 128], F32, kind="ExternalInput")
    ident_b = nc.dram_tensor("ident_b", [128, 128], BF16, kind="ExternalInput")
    y_out = nc.dram_tensor("y", [T, D], F32, kind="ExternalOutput")
    if DEBUG:
        dbg_ctx = nc.dram_tensor("dbg_ctx", [T, D], BF16, kind="ExternalOutput")
        dbg_q = nc.dram_tensor("dbg_q", [D, T], BF16, kind="ExternalOutput")
        dbg_kt = nc.dram_tensor("dbg_kt", [NC * D, T], BF16, kind="ExternalOutput")
        dbg_vr = nc.dram_tensor("dbg_vr", [S, D], BF16, kind="ExternalOutput")
        dbg_vin = nc.dram_tensor("dbg_vin", [T, D], BF16, kind="ExternalOutput")
        dbg_acc = nc.dram_tensor("dbg_acc", [128, 520], F32, kind="ExternalOutput")
        dbg_vp = nc.dram_tensor("dbg_vp", [128, KCN * 130], BF16, kind="ExternalOutput")
        dbg_m = nc.dram_tensor("dbg_m", [128, 96], F32, kind="ExternalOutput")
        dbg_kn = nc.dram_tensor("dbg_kn", [NC * HEADS, T], F32, kind="ExternalOutput")
        dbg_bx = nc.dram_tensor("dbg_bx", [NBAND * 640 + 64], BF16, kind="ExternalOutput")

    pgu_out = nc.dram_tensor("pgu_out", [2 * HCP * 128, D], BF16)
    pgu_all = nc.dram_tensor("pgu_all", [NC * 2 * HCP * 128, D], BF16)
    pd_out = nc.dram_tensor("pd_out", [256, H], BF16)
    pd_all = nc.dram_tensor("pd_all", [2048, H], BF16)
    pa_out = nc.dram_tensor("pa_out", [1024, D], BF16)
    pa_all = nc.dram_tensor("pa_all", [8192, D], BF16)
    kt_in = nc.dram_tensor("kt_in", [D, T], BF16)
    kt_out = nc.dram_tensor("kt_out", [NC * D, T], BF16)
    v_in = nc.dram_tensor("v_in", [T, D], BF16)
    v_out = nc.dram_tensor("v_out", [S, D], BF16)
    v_rot = nc.dram_tensor("v_rot", [S, D], BF16)
    kn_in = nc.dram_tensor("kn_in", [HEADS, T], F32)
    kn_out = nc.dram_tensor("kn_out", [NC * HEADS, T], F32)
    qn_dram = nc.dram_tensor("qn_dram", [HEADS, T], F32)
    q_dram = nc.dram_tensor("q_dram", [D, T], BF16)
    bexp_dram = nc.dram_tensor("bexp_dram", [NBAND * 640 + 64], BF16)
    ctx_dram = nc.dram_tensor("ctx_dram", [T, D], BF16)

    k = KB(nc)
    for s in ("sq", "cc", "T", "V", "S", "wsA", "wsB", "kp", "vp",
              "icl", "msc", "xA", "xB", "pA", "pB", "psA2", "psB2", "rl", "bx",
              "ksA", "ksB", "vtA", "vtB", "nst", "vrA", "vrB", "gkb",
              "ctA", "ctB", "cxA", "cxB", "oqA", "oqB", "idx"):
        k.new_sem(stack, s)

    sb = nc.sbuf_tensor
    XT = [stack.enter_context(sb(f"xt{i}", [128, T], F32)) for i in range(DC)]
    LN1 = [stack.enter_context(sb(f"ln{i}", [128, TH], BF16)) for i in range(DC)]
    idf = stack.enter_context(sb("idf", [128, 128], F32))
    idb = stack.enter_context(sb("idb", [128, 128], BF16))
    ones_col_b = stack.enter_context(sb("ones_col_b", [128, 1], BF16))
    w64_col = stack.enter_context(sb("w64_col", [128, 1], BF16))
    mask_top = stack.enter_context(sb("mask_top", [128, 1], BF16))
    mask_bot = stack.enter_context(sb("mask_bot", [128, 1], BF16))
    ones_row_b = stack.enter_context(sb("ones_row_b", [1, 128], BF16))
    ones_row_f = stack.enter_context(sb("ones_row_f", [1, 128], F32))
    brow_ps = stack.enter_context(sb("brow_ps", [1, NP_P + NP_S], F32))
    bmax_sb = stack.enter_context(sb("bmax_sb", [1, 1], F32))
    rowA = stack.enter_context(sb("rowA", [1, TH], F32))
    rowB = stack.enter_context(sb("rowB", [1, TH], F32))
    rowC = stack.enter_context(sb("rowC", [1, TH], F32))
    rowBF = stack.enter_context(sb("rowBF", [1, TH], BF16))
    rowBF2 = stack.enter_context(sb("rowBF2", [1, TH], BF16))
    xbt = stack.enter_context(sb("xbt", [128, TH], BF16))
    xsq = stack.enter_context(sb("xsq", [128, TH], BF16))
    tmpf = stack.enter_context(sb("tmpf", [128, TH], F32))
    Abc = stack.enter_context(sb("Abc", [128, TH], BF16))
    Bbc = stack.enter_context(sb("Bbc", [128, TH], BF16))

    ps_a = stack.enter_context(nc.psum_tensor([128, 1024], F32))
    ps_b = stack.enter_context(nc.psum_tensor([128, 1024], F32))
    ps_c = stack.enter_context(nc.psum_tensor([128, 1024], F32))
    ps_d = stack.enter_context(nc.psum_tensor([128, 1024], F32))

    def psA(h):
        return ps_a[:, h * 512: h * 512 + TH]

    def psB(h):
        return ps_b[:, h * 512: h * 512 + TH]

    def psC(h, w=TH):
        return ps_c[:, h * 512: h * 512 + w]

    def psD(h, w=TH):
        return ps_d[:, h * 512: h * 512 + w]

    k.op("sync", lambda e: e.dma_start(out=idf[:], in_=ident_f[:, :]), inc=("icl", 16))
    k.op("sync", lambda e: e.dma_start(out=idb[:], in_=ident_b[:, :]), inc=("icl", 16))
    id_sq = k.cnt["icl"]
    for buf, val in ((ones_col_b, 1.0), (w64_col, 1.0 / 64), (ones_row_b, 1.0),
                     (ones_row_f, 1.0)):
        k.op("vector", lambda e, b=buf, v=val: e.memset(b[:], v), inc=("V", 1))
    k.op("vector", lambda e: e.memset(mask_top[:], 0.0), inc=("V", 1))
    k.op("vector", lambda e: e.memset(mask_top[0:64, :], 1.0), inc=("V", 1))
    k.op("vector", lambda e: e.memset(mask_bot[:], 0.0), inc=("V", 1))
    k.op("vector", lambda e: e.memset(mask_bot[64:128, :], 1.0), inc=("V", 1))
    consts_V = k.cnt["V"]

    sc1 = contextlib.ExitStack()
    xstage = [sc1.enter_context(sb(f"xs{i}", [128, D], F32)) for i in range(2)]
    pin = [sc1.enter_context(sb(f"pin{i}", [128, HW], I32)) for i in range(2)]
    pdq = [sc1.enter_context(sb(f"pdq{i}", [128, HW], BF16)) for i in range(2)]
    pstr = [sc1.enter_context(sb(f"pstr{i}", [128, HW], BF16)) for i in range(2)]
    scg = sc1.enter_context(sb("scg", [128, HCP], F32))
    scu = sc1.enter_context(sb("scu", [128, HCP], F32))
    scd = sc1.enter_context(sb("scd", [128, 2], F32))
    relc = sc1.enter_context(sb("relc", [64, 2048], F32))
    relb = sc1.enter_context(sb("relb", [64, 2048], BF16))
    bchunk = sc1.enter_context(sb("bchunk", [1, 2048], F32))
    bexpc = sc1.enter_context(sb("bexpc", [1, 2048], BF16))

    ev_hist = []
    for tt in range(8):
        slot = tt % 2
        xsem = "xA" if slot == 0 else "xB"
        lw = k.op("sync", lambda e, tt=tt, slot=slot:
                  e.dma_start(out=xstage[slot][:], in_=x_in[tt * 128:(tt + 1) * 128, :]),
                  inc=(xsem, 16),
                  wait=[("icl", id_sq)] + ([("V", ev_hist[tt - 2])] if tt >= 2 else []))
        for dc in range(DC):
            h = dc % 2
            tT = k.op("tensor", lambda e, slot=slot, dc=dc, h=h:
                      e.transpose(psC(h, 128), xstage[slot][:, dc * 128:(dc + 1) * 128],
                                  idf[:]),
                      inc=("T", 1),
                      wait=[(xsem, lw)] + ([("V", k.cnt["V"])] if (tt or dc) else []))
            k.op("vector", lambda e, dc=dc, tt=tt, h=h:
                 e.tensor_copy(XT[dc][:, tt * 128:(tt + 1) * 128], psC(h, 128)),
                 inc=("V", 1), wait=[("T", tT)])
        ev_hist.append(k.cnt["V"])

    k.op("sync", lambda e: e.dma_start(out=scg[:], in_=sg_sl[:, :]), inc=("msc", 16))
    k.op("sync", lambda e: e.dma_start(out=scu[:], in_=su_sl[:, :]), inc=("msc", 16))
    k.op("sync", lambda e: e.dma_start(out=scd[:], in_=sd_sl[:, :]), inc=("msc", 16))
    sc_sq = k.cnt["msc"]
    k.op("vector", lambda e: e.tensor_scalar_mul(scd[:], scd[:], 0.5), inc=("V", 1),
         wait=[("msc", sc_sq)])
    prep_sc_V = k.cnt["V"]

    prep_dq = [0, 0]
    prep_st = [0, 0]

    def prep_chunk(load_fn, width, dcw, dequant, sc, sc_col, store_fn, li):
        slot = li % 2
        psem = "pA" if slot == 0 else "pB"
        ssem = "psA2" if slot == 0 else "psB2"
        lw = k.op("sync", lambda e, slot=slot: load_fn(e, slot),
                  inc=(psem, 16), wait=[("V", prep_dq[slot])])
        if dequant:
            dqv = k.op("vector", lambda e, slot=slot:
                       e.tensor_scalar(pdq[slot][:, 0:width], pin[slot][:, 0:width],
                                       -128.0, sc[:, sc_col:sc_col + 1],
                                       ALU.add, ALU.mult),
                       inc=("V", 1), wait=[(psem, lw), ("V", prep_sc_V)])
        else:
            dqv = k.op("vector", lambda e, slot=slot:
                       e.tensor_copy(pdq[slot][:, 0:width],
                                     pin[slot].bitcast(F32)[:, 0:width]),
                       inc=("V", 1), wait=[(psem, lw)])
        prep_dq[slot] = dqv
        for dc in range(dcw):
            h = dc % 2
            tT = k.op("tensor", lambda e, slot=slot, dc=dc, h=h:
                      e.transpose(psD(h, 64).bitcast(BF16),
                                  pdq[slot][:, dc * 128:(dc + 1) * 128], idb[:]),
                      inc=("T", 1), wait=[("V", dqv), ("V", k.cnt["V"])])
            k.op("vector", lambda e, slot=slot, dc=dc, h=h, ssem=ssem:
                 e.tensor_copy(pstr[slot][:, dc * 128:(dc + 1) * 128],
                               psD(h, 64).bitcast(BF16)),
                 inc=("V", 1), wait=[("T", tT), (ssem, prep_st[slot])])
        sv = k.cnt["V"]
        prep_st[slot] = k.op("gpsimd", lambda e, slot=slot: store_fn(e, slot),
                             inc=(ssem, 16), wait=[("V", sv)])

    li = 0
    for l in range(HCP):
        prep_chunk(lambda e, slot, l=l: e.dma_start(
            out=pin[slot][:, 0:2048], in_=wg_sl[l * 128:(l + 1) * 128, :]),
            2048, 16, True, scg, l,
            lambda e, slot, l=l: e.dma_start(
                out=pgu_out[l * 128:(l + 1) * 128, :], in_=pstr[slot][:, 0:2048]), li)
        li += 1
    for l in range(HCP):
        prep_chunk(lambda e, slot, l=l: e.dma_start(
            out=pin[slot][:, 0:2048], in_=wu_sl[l * 128:(l + 1) * 128, :]),
            2048, 16, True, scu, l,
            lambda e, slot, l=l: e.dma_start(
                out=pgu_out[(HCP + l) * 128:(HCP + l + 1) * 128, :],
                in_=pstr[slot][:, 0:2048]), li)
        li += 1
    for l in range(2):
        for hh in range(2):
            prep_chunk(lambda e, slot, l=l, hh=hh: e.dma_start(
                out=pin[slot][:, 0:HW],
                in_=wd_sl[l * 128:(l + 1) * 128, hh * HW:(hh + 1) * HW]),
                HW, HWC, True, scd, l,
                lambda e, slot, l=l, hh=hh: e.dma_start(
                    out=pd_out[l * 128:(l + 1) * 128, hh * HW:(hh + 1) * HW],
                    in_=pstr[slot][:, 0:HW]), li)
            li += 1
    for l in range(8):
        prep_chunk(lambda e, slot, l=l: e.dma_start(
            out=pin[slot].bitcast(F32)[:, 0:2048], in_=wa_sl[l * 128:(l + 1) * 128, :]),
            2048, 16, False, None, 0,
            lambda e, slot, l=l: e.dma_start(
                out=pa_out[l * 128:(l + 1) * 128, :], in_=pstr[slot][:, 0:2048]), li)
        li += 1
    prep_bar = [("psA2", k.cnt["psA2"]), ("psB2", k.cnt["psB2"])]

    cc_gu = k.op("gpsimd", lambda e: e.collective_compute(
        "AllGather", ALU.bypass, replica_groups=[list(range(NC))],
        ins=[pgu_out.ap().opt()], outs=[pgu_all.ap().opt()]),
        inc=("cc", 1), wait=prep_bar)
    cc_d = k.op("gpsimd", lambda e: e.collective_compute(
        "AllGather", ALU.bypass, replica_groups=[list(range(NC))],
        ins=[pd_out.ap().opt()], outs=[pd_all.ap().opt()]),
        inc=("cc", 1))
    cc_a = k.op("gpsimd", lambda e: e.collective_compute(
        "AllGather", ALU.bypass, replica_groups=[list(range(NC))],
        ins=[pa_out.ap().opt()], outs=[pa_all.ap().opt()]),
        inc=("cc", 1))

    rel_cv = rel_mm = rel_exp = rel_dma = 0
    nchunks = (NTAB + 2047) // 2048
    for ci in range(nchunks):
        c0 = ci * 2048
        cw = min(2048, NTAB - c0)
        lw = k.op("sync", lambda e, c0=c0, cw=cw:
                  e.dma_start(out=relc[:, 0:cw], in_=rel_tabt[:, c0:c0 + cw]),
                  inc=("rl", 16), wait=[("V", rel_cv)])
        rel_cv = k.op("vector", lambda e, cw=cw: e.tensor_copy(relb[:, 0:cw], relc[:, 0:cw]),
                      inc=("V", 1), wait=[("rl", lw), ("V", consts_V), ("T", rel_mm)])
        for n0 in range(0, cw, 512):
            nw = min(512, cw - n0)
            h = (n0 // 512) % 2
            rel_mm = k.op("tensor", lambda e, n0=n0, nw=nw, h=h:
                          e.matmul(ps_c[0:1, h * 512: h * 512 + nw],
                                   w64_col[0:64, :], relb[0:64, n0:n0 + nw],
                                   start=True, stop=True),
                          inc=("T", 1), wait=[("V", rel_cv), ("V", k.cnt["V"])])
            k.op("vector", lambda e, n0=n0, nw=nw, h=h:
                 e.tensor_copy(bchunk[:, n0:n0 + nw], ps_c[0:1, h * 512: h * 512 + nw]),
                 inc=("V", 1), wait=[("T", rel_mm), ("S", rel_exp)])
        if c0 < NP_P + NP_S:
            w = min(cw, NP_P + NP_S - c0)
            k.op("vector", lambda e, c0=c0, w=w:
                 e.tensor_copy(brow_ps[:, c0:c0 + w], bchunk[:, 0:w]), inc=("V", 1),
                 wait=[("V", k.cnt["V"] - 1)])
        if c0 + cw > BAND0:
            b0 = max(BAND0, c0)
            rel_exp = k.op("scalar", lambda e, c0=c0, b0=b0, cw=cw:
                           e.activation(out=bexpc[:, b0 - c0:cw], in_=bchunk[:, b0 - c0:cw],
                                        func=AF.Exp),
                           inc=("S", 1), wait=[("V", k.cnt["V"]), ("bx", rel_dma)])
            rel_dma = k.op("gpsimd", lambda e, c0=c0, b0=b0, cw=cw:
                           e.dma_start(out=dview(bexp_dram, b0 - BAND0,
                                                 [[0, 1], [1, c0 + cw - b0]]),
                                       in_=bexpc[0:1, b0 - c0:cw]),
                           inc=("bx", 16), wait=[("S", rel_exp)])
    k.op("vector", lambda e: e.reduce_max(bmax_sb[:], brow_ps[:, 0:1023], AX.X,
                                          apply_absolute_value=True), inc=("V", 1),
         wait=[("V", k.cnt["V"] - 1)])
    relprep_V = k.cnt["V"]
    relprep_gq = k.cnt["bx"]
    sc1.close()

    def emit_ln(th):
        t0 = th * TH
        sq_prev = 0
        for dc in range(DC):
            cv = k.op("vector", lambda e, dc=dc, t0=t0:
                      e.tensor_copy(xbt[:], XT[dc][:, t0:t0 + TH]), inc=("V", 1),
                      wait=[("T", k.cnt["T"]), ("S", sq_prev)])
            sq_prev = k.op("scalar", lambda e:
                           e.activation(out=xsq[:], in_=xbt[:], func=AF.Square),
                           inc=("S", 1), wait=[("V", cv), ("T", k.cnt["T"])])
            k.op("tensor", lambda e, dc=dc:
                 e.matmul(ps_d[0:1, 0:TH], ones_col_b[:], xbt[:],
                          start=(dc == 0), stop=(dc == DC - 1)),
                 inc=("T", 1),
                 wait=[("V", cv)] + ([("V", k.cnt["V"])] if dc == 0 else []))
            k.op("tensor", lambda e, dc=dc:
                 e.matmul(ps_d[0:1, 512:512 + TH], ones_col_b[:], xsq[:],
                          start=(dc == 0), stop=(dc == DC - 1)),
                 inc=("T", 1), wait=[("S", sq_prev)])
        tS = k.cnt["T"]
        va1 = k.op("vector", lambda e: e.tensor_scalar_mul(rowA[:], ps_d[0:1, 0:TH], 1.0 / D),
                   inc=("V", 1), wait=[("T", tS)])
        vb1 = k.op("vector", lambda e: e.tensor_scalar_mul(rowB[:], ps_d[0:1, 512:512 + TH],
                                                           1.0 / D), inc=("V", 1))
        vc1 = k.op("vector", lambda e: e.tensor_mul(rowC[:], rowA[:], rowA[:]), inc=("V", 1),
                   wait=[("V", va1)])
        v4 = k.op("vector", lambda e: e.scalar_tensor_tensor(
            rowB[:], rowB[:], float(EPS), rowC[:], ALU.add, ALU.subtract), inc=("V", 1),
            wait=[("V", vc1)])
        s1 = k.op("scalar", lambda e: e.activation(out=rowB[:], in_=rowB[:], func=AF.Ln),
                  inc=("S", 1), wait=[("V", v4)])
        s2 = k.op("scalar", lambda e: e.activation(out=rowB[:], in_=rowB[:], func=AF.Exp,
                                                   scale=-0.5), inc=("S", 1),
                  wait=[("S", s1)])
        vbf = k.op("vector", lambda e: e.tensor_copy(rowBF[:], rowB[:]), inc=("V", 1),
                   wait=[("S", s2)])
        vc2 = k.op("vector", lambda e: e.tensor_mul(rowC[:], rowA[:], rowB[:]), inc=("V", 1))
        vc3 = k.op("vector", lambda e: e.tensor_scalar_mul(rowC[:], rowC[:], -1.0),
                   inc=("V", 1), wait=[("V", vc2)])
        v8 = k.op("vector", lambda e: e.tensor_copy(rowBF2[:], rowC[:]), inc=("V", 1),
                  wait=[("V", vc3)])
        tA = k.op("tensor", lambda e: e.matmul(ps_d[:, 0:TH], ones_row_b[:], rowBF[:],
                                               start=True, stop=True),
                  inc=("T", 1), wait=[("V", v8)])
        vA = k.op("vector", lambda e: e.tensor_copy(Abc[:], ps_d[:, 0:TH]), inc=("V", 1),
                  wait=[("T", tA)])
        tB = k.op("tensor", lambda e: e.matmul(ps_d[:, 512:512 + TH], ones_row_b[:],
                                               rowBF2[:], start=True, stop=True),
                  inc=("T", 1), wait=[("V", vA)])
        k.op("vector", lambda e: e.tensor_copy(Bbc[:], ps_d[:, 512:512 + TH]), inc=("V", 1),
             wait=[("T", tB)])
        for dc in range(DC):
            vt1 = k.op("vector", lambda e, dc=dc, t0=t0:
                       e.tensor_mul(tmpf[:], XT[dc][:, t0:t0 + TH], Abc[:]), inc=("V", 1))
            k.op("vector", lambda e, dc=dc:
                 e.tensor_add(LN1[dc][:, 0:TH], tmpf[:], Bbc[:]), inc=("V", 1),
                 wait=[("V", vt1)])
        return k.cnt["V"]

    def emit_ffn(first_app, wstr, Z, SG):
        silu_hist = {}
        z_hist = {}
        strip_t = {"wsA": k.cnt["T"], "wsB": k.cnt["T"]}
        ffn_entry_V = k.cnt["V"]
        for th in range(2):
            lnv = emit_ln(th)
            for hc in range(HC):
                slot = hc % 2
                ssem = "wsA" if slot == 0 else "wsB"
                r, l = hc // HCP, hc % HCP
                g_row = (r * 2 * HCP + l) * 128
                u_row = (r * 2 * HCP + HCP + l) * 128
                wcc = [("cc", cc_gu)] if (first_app and th == 0 and hc < 2) else []
                lg = k.op("sync", lambda e, g_row=g_row, slot=slot:
                          e.dma_start(out=wstr[slot][:, 0:2048],
                                      in_=pgu_all[g_row:g_row + 128, :]),
                          inc=(ssem, 16), wait=wcc + [("T", strip_t[ssem])])
                lu = k.op("sync", lambda e, u_row=u_row, slot=slot:
                          e.dma_start(out=wstr[slot][:, 2048:4096],
                                      in_=pgu_all[u_row:u_row + 128, :]),
                          inc=(ssem, 16))
                h = hc % 2
                for dc in range(DC):
                    w0 = None
                    if dc == 0:
                        w0 = [(ssem, lu), ("V", lnv),
                              ("S", silu_hist.get((th, hc - 2), 0))]
                        if th == 0 and hc < 2:
                            w0.append(("V", ffn_entry_V))
                    k.op("tensor", lambda e, slot=slot, dc=dc, h=h:
                         e.matmul(psA(h), wstr[slot][:, dc * 128:(dc + 1) * 128],
                                  LN1[dc][:, 0:TH],
                                  start=(dc == 0), stop=(dc == DC - 1)),
                         inc=("T", 1) if dc == DC - 1 else None, wait=w0)
                tg = k.cnt["T"]
                for dc in range(DC):
                    w0 = ([(ssem, lu), ("V", z_hist.get((th, hc - 2), 0))]
                          if dc == 0 else None)
                    k.op("tensor", lambda e, slot=slot, dc=dc, h=h:
                         e.matmul(psB(h),
                                  wstr[slot][:, 2048 + dc * 128: 2048 + (dc + 1) * 128],
                                  LN1[dc][:, 0:TH],
                                  start=(dc == 0), stop=(dc == DC - 1)),
                         inc=("T", 1) if dc == DC - 1 else None, wait=w0)
                tu = k.cnt["T"]
                strip_t[ssem] = tu
                sg2 = k.op("scalar", lambda e, h=h:
                           e.activation(out=SG[:], in_=psA(h), func=AF.Silu),
                           inc=("S", 1),
                           wait=[("T", tg), ("V", z_hist.get((th, hc - 1), 0))])
                silu_hist[(th, hc)] = sg2
                zv = k.op("vector", lambda e, hc=hc, h=h:
                          e.tensor_tensor(Z[:, hc * TH:(hc + 1) * TH], psB(h), SG[:],
                                          ALU.mult),
                          inc=("V", 1), wait=[("T", tu), ("S", sg2)])
                z_hist[(th, hc)] = zv
            zall = k.cnt["V"]
            for dmc in range(DC):
                slot = dmc % 2
                ssem = "wsA" if slot == 0 else "wsB"
                wcc = [("cc", cc_d)] if (first_app and th == 0 and dmc < 2) else []
                ld = k.op("sync", lambda e, dmc=dmc, slot=slot:
                          e.dma_start(out=wstr[slot][:, 0:H],
                                      in_=pd_all[dmc * 128:(dmc + 1) * 128, :]),
                          inc=(ssem, 16), wait=wcc + [("T", strip_t[ssem])])
                h = dmc % 2
                for hcc in range(HC):
                    w0 = ([(ssem, ld), ("V", zall), ("V", k.cnt["V"])]
                          if hcc == 0 else None)
                    k.op("tensor", lambda e, slot=slot, hcc=hcc, h=h:
                         e.matmul(psC(h), wstr[slot][:, hcc * 128:(hcc + 1) * 128],
                                  Z[:, hcc * TH:(hcc + 1) * TH],
                                  start=(hcc == 0), stop=(hcc == HC - 1)),
                         inc=("T", 1) if hcc == HC - 1 else None, wait=w0)
                td = k.cnt["T"]
                strip_t[ssem] = td
                k.op("vector", lambda e, dmc=dmc, h=h, th=th:
                     e.tensor_add(XT[dmc][:, th * TH:(th + 1) * TH], psC(h),
                                  XT[dmc][:, th * TH:(th + 1) * TH]),
                     inc=("V", 1), wait=[("T", td)])

    if STAGE >= 1:
        sc_f1 = contextlib.ExitStack()
        wstr1 = [sc_f1.enter_context(sb(f"w1s{i}", [128, H], BF16)) for i in range(2)]
        Z1 = sc_f1.enter_context(sb("Z1", [128, HC * TH], BF16))
        SG1 = sc_f1.enter_context(sb("SG1", [128, TH], BF16))
        emit_ffn(True, wstr1, Z1, SG1)
        sc_f1.close()

    if STAGE >= 2:
        sc2 = contextlib.ExitStack()
        idxs = sc2.enter_context(sb("idxs", [128, 192], I32))
        Mneg = sc2.enter_context(sb("Mneg", [128, 32], F32))
        Bsat = sc2.enter_context(sb("Bsat", [128, 2048], F32))
        band = sc2.enter_context(sb("band", [128, NBAND * 512], BF16))

        scD = contextlib.ExitStack()
        kstage = [scD.enter_context(sb(f"kst{i}", [128, TH], BF16)) for i in range(2)]
        vstage = [scD.enter_context(sb(f"vst{i}", [128, TH], BF16)) for i in range(2)]
        vtt = [scD.enter_context(sb(f"vtt{i}", [128, 128], BF16)) for i in range(2)]
        nstage = scD.enter_context(sb("nstage", [1, TH], F32))
        nsq = scD.enter_context(sb("nsq", [128, TH], BF16))
        wstrA = [scD.enter_context(sb(f"wsa{i}", [128, 2048], BF16)) for i in range(2)]
        vr = [scD.enter_context(sb(f"vr{i}", [128, D], BF16)) for i in range(2)]
        gkbuf = scD.enter_context(sb("gkbuf", [32, 2048], F32))
        gk2 = scD.enter_context(sb("gk2", [32, 1], F32))
        gq2 = scD.enter_context(sb("gq2", [32, 1], F32))
        gtmp = scD.enter_context(sb("gtmp", [32, 1], F32))
        mrow = scD.enter_context(sb("mrow", [1, 32], F32))
        bmax_bc = scD.enter_context(sb("bmax_bc", [32, 1], F32))
        brow_sat = scD.enter_context(sb("brow_sat", [1, 2048], F32))

        strip_t = {"wsA": k.cnt["T"], "wsB": k.cnt["T"]}
        ev_q = 0
        ev_v = 0
        vstage_t = [0, 0]
        vtt_g = [0, 0]
        kst_g = [0, 0]
        qkv_entry_V = k.cnt["V"]
        for th in range(2):
            lnv = emit_ln(th)
            for w in range(3):
                for ec in range(ECN):
                    slot = ec % 2
                    ssem = "wsA" if slot == 0 else "wsB"
                    r, l = ec // 2, ec % 2
                    row = (r * 4 + w) * 256 + l * 128
                    wcc = [("cc", cc_a)] if (th == 0 and w == 0 and ec < 2) else []
                    lw = k.op("sync", lambda e, row=row, slot=slot:
                              e.dma_start(out=wstrA[slot][:], in_=pa_all[row:row + 128, :]),
                              inc=(ssem, 16), wait=wcc + [("T", strip_t[ssem])])
                    h = ec % 2
                    for dc in range(DC):
                        w0 = None
                        if dc == 0:
                            w0 = [(ssem, lw), ("V", lnv), ("S", ev_q), ("V", ev_v)]
                            if th == 0 and w == 0 and ec < 2:
                                w0.append(("V", qkv_entry_V))
                        k.op("tensor", lambda e, slot=slot, dc=dc, h=h:
                             e.matmul(psA(h), wstrA[slot][:, dc * 128:(dc + 1) * 128],
                                      LN1[dc][:, 0:TH],
                                      start=(dc == 0), stop=(dc == DC - 1)),
                             inc=("T", 1) if dc == DC - 1 else None, wait=w0)
                    tq = k.cnt["T"]
                    if w < 2:
                        qk_dst = q_dram if w == 0 else kt_in
                        nrm_dst = qn_dram if w == 0 else kn_in
                        kss = "ksA" if slot == 0 else "ksB"
                        ev_q = k.op("scalar", lambda e, slot=slot, h=h, kss=kss:
                                    e.activation(out=kstage[slot][:], in_=psA(h),
                                                 func=AF.Copy),
                                    inc=("S", 1), wait=[("T", tq), (kss, kst_g[slot])])
                        kst_g[slot] = k.op(
                            "gpsimd", lambda e, ec=ec, th=th, slot=slot, qk_dst=qk_dst:
                            e.dma_start(out=qk_dst[ec * 128:(ec + 1) * 128,
                                                   th * TH:(th + 1) * TH],
                                        in_=kstage[slot][:]),
                            inc=(kss, 16), wait=[("S", ev_q)])
                        sq2 = k.op("scalar", lambda e, slot=slot:
                                   e.activation(out=nsq[:], in_=kstage[slot][:],
                                                func=AF.Square),
                                   inc=("S", 1), wait=[("T", k.cnt["T"]), ("S", ev_q)])
                    if w < 2:
                        for hh in range(2):
                            mask = mask_top if hh == 0 else mask_bot
                            head = ec * 2 + hh
                            tn = k.op("tensor", lambda e, mask=mask:
                                      e.matmul(ps_d[0:1, 0:TH], mask[:], nsq[:],
                                               start=True, stop=True),
                                      inc=("T", 1), wait=[("S", sq2), ("V", k.cnt["V"])])
                            vn = k.op("vector", lambda e:
                                      e.tensor_copy(nstage[:], ps_d[0:1, 0:TH]),
                                      inc=("V", 1), wait=[("T", tn), ("nst", k.cnt["nst"])])
                            k.op("gpsimd", lambda e, head=head, th=th, nd=nrm_dst:
                                 e.dma_start(out=nd[head:head + 1, th * TH:(th + 1) * TH],
                                             in_=nstage[:]),
                                 inc=("nst", 16), wait=[("V", vn)])
                    if w == 2:
                        ev_v = k.op("vector", lambda e, slot=slot, h=h:
                                    e.tensor_copy(vstage[slot][:], psA(h)),
                                    inc=("V", 1), wait=[("T", tq), ("T", vstage_t[slot])])
                        for tt in range(4):
                            h2 = tt % 2
                            tv = k.op("tensor", lambda e, slot=slot, tt=tt, h2=h2:
                                      e.transpose(psC(h2, 64).bitcast(BF16),
                                                  vstage[slot][:, tt * 128:(tt + 1) * 128],
                                                  idb[:]),
                                      inc=("T", 1), wait=[("V", ev_v), ("V", k.cnt["V"])])
                            vstage_t[slot] = tv
                            ti = tt % 2
                            vts = "vtA" if ti == 0 else "vtB"
                            vt = k.op("vector", lambda e, ti=ti, h2=h2, vts=vts:
                                      e.tensor_copy(vtt[ti][:], psC(h2, 64).bitcast(BF16)),
                                      inc=("V", 1), wait=[("T", tv), (vts, vtt_g[ti])])
                            vtt_g[ti] = k.op(
                                "gpsimd",
                                lambda e, ec=ec, th=th, tt=tt, ti=ti:
                                e.dma_start(out=v_in[th * TH + tt * 128:
                                                     th * TH + (tt + 1) * 128,
                                                     ec * 128:(ec + 1) * 128],
                                            in_=vtt[ti][:]),
                                inc=(vts, 16), wait=[("V", vt)])
                    strip_t[ssem] = k.cnt["T"]
        qkv_bar = [("ksA", k.cnt["ksA"]), ("ksB", k.cnt["ksB"]),
                   ("vtA", k.cnt["vtA"]), ("vtB", k.cnt["vtB"]),
                   ("nst", k.cnt["nst"])]
        cc_kt = k.op("gpsimd", lambda e: e.collective_compute(
            "AllGather", ALU.bypass, replica_groups=[list(range(NC))],
            ins=[kt_in.ap().opt()], outs=[kt_out.ap().opt()]),
            inc=("cc", 1), wait=qkv_bar)
        cc_v = k.op("gpsimd", lambda e: e.collective_compute(
            "AllGather", ALU.bypass, replica_groups=[list(range(NC))],
            ins=[v_in.ap().opt()], outs=[v_out.ap().opt()]),
            inc=("cc", 1))
        cc_kn = k.op("gpsimd", lambda e: e.collective_compute(
            "AllGather", ALU.bypass, replica_groups=[list(range(NC))],
            ins=[kn_in.ap().opt()], outs=[kn_out.ap().opt()]),
            inc=("cc", 1))

        ld_idx = k.op("sync", lambda e: e.dma_start(out=idxs[:], in_=idx_tab[:, :]),
                      inc=("idx", 16))
        vr_store = [0, 0]
        for kc in range(KCN):
            slot = kc % 2
            vrs = "vrA" if slot == 0 else "vrB"
            gv = k.op("gpsimd", lambda e, kc=kc, slot=slot:
                      e.indirect_dma_start(
                          vr[slot][:, :], None, v_out[:, :],
                          IndirectOffsetOnAxis(idxs[:, 128 + kc:129 + kc], 0)),
                      inc=(vrs, 16),
                      wait=([("cc", cc_v), ("idx", ld_idx)] if kc == 0
                            else [(vrs, vr_store[slot])]))
            vr_store[slot] = k.op(
                "gpsimd", lambda e, kc=kc, slot=slot:
                e.dma_start(out=v_rot[kc * 128:(kc + 1) * 128, :], in_=vr[slot][:]),
                inc=(vrs, 16), wait=[(vrs, gv)])
        vrot_bar = [("vrA", k.cnt["vrA"]), ("vrB", k.cnt["vrB"])]

        vk = k.op("vector", lambda e: e.memset(gk2[:], 0.0), inc=("V", 1))
        for part in range(4):
            lkn = k.op("sync", lambda e, part=part: e.dma_start(
                out=gkbuf[:].rearrange("h (r t) -> h r t", r=2),
                in_=dview(kn_out, part * 2 * HEADS * T,
                          [[T, 32], [HEADS * T, 2], [1, T]])),
                inc=("gkb", 16), wait=[("cc", cc_kn), ("V", vk)])
            vred = k.op("vector", lambda e: e.reduce_max(gtmp[:], gkbuf[:], AX.X),
                        inc=("V", 1), wait=[("gkb", lkn)])
            vk = k.op("vector", lambda e: e.tensor_max(gk2[:], gk2[:], gtmp[:]),
                      inc=("V", 1), wait=[("V", vred)])
        lqn = k.op("sync", lambda e: e.dma_start(
            out=gkbuf[:, 0:T], in_=qn_dram[:, :]),
            inc=("gkb", 16), wait=[("V", vk)] + qkv_bar)
        vq = k.op("vector", lambda e: e.reduce_max(gq2[:], gkbuf[:, 0:T], AX.X),
                  inc=("V", 1), wait=[("gkb", lqn)])
        tb = k.op("tensor", lambda e: e.matmul(ps_d[0:32, 0:1], ones_row_f[:, 0:32],
                                               bmax_sb[:], start=True, stop=True),
                  inc=("T", 1), wait=[("V", relprep_V), ("V", k.cnt["V"])])
        vbm = k.op("vector", lambda e: e.tensor_copy(bmax_bc[:], ps_d[0:32, 0:1]),
                   inc=("V", 1), wait=[("T", tb)])
        vs1 = k.op("vector", lambda e: e.tensor_add(gtmp[:], gq2[:], gk2[:]), inc=("V", 1),
                   wait=[("V", vq), ("V", vbm)])
        vM = k.op("vector", lambda e: e.scalar_tensor_tensor(
            gq2[:], gtmp[:], -1.0 / 16, bmax_bc[:], ALU.mult, ALU.subtract),
            inc=("V", 1), wait=[("V", vs1)])
        tM = k.op("tensor", lambda e: e.transpose(ps_d[0:1, 0:32], gq2[:],
                                                  idf[0:32, 0:32]),
                  inc=("T", 1), wait=[("V", vM)])
        vMr = k.op("vector", lambda e: e.tensor_copy(mrow[:], ps_d[0:1, 0:32]),
                   inc=("V", 1), wait=[("T", tM)])
        tMn = k.op("tensor", lambda e: e.matmul(ps_d[:, 0:32], ones_row_f[:], mrow[:],
                                                start=True, stop=True),
                   inc=("T", 1), wait=[("V", vMr)])
        k.op("vector", lambda e: e.tensor_copy(Mneg[:], ps_d[:, 0:32]),
             inc=("V", 1), wait=[("T", tMn)])
        for hh in range(HEADS):
            k.op("vector", lambda e, hh=hh: e.tensor_scalar_add(
                fview(brow_sat, hh * 64, [[32, 2], [1, 32]]),
                fview(brow_ps, NP_P, [[64, 2], [2, 32]]),
                mrow[0:1, hh:hh + 1]), inc=("V", 1),
                wait=[("V", vMr)] if hh == 0 else None)
        for n0 in range(0, 2048, 512):
            h = (n0 // 512) % 2
            tB2 = k.op("tensor", lambda e, n0=n0, h=h:
                       e.matmul(psD(h), ones_row_f[:], brow_sat[0:1, n0:n0 + 512],
                                start=True, stop=True),
                       inc=("T", 1), wait=[("V", k.cnt["V"])])
            k.op("vector", lambda e, n0=n0, h=h:
                 e.tensor_copy(Bsat[:, n0:n0 + 512], psD(h)),
                 inc=("V", 1), wait=[("T", tB2)])

        if DEBUG:
            k.op("gpsimd", lambda e: e.dma_start(out=dbg_m.ap().ap and dbg_m[0:128, 0:32], in_=Mneg[:]),
                 inc=("oqA", 16), wait=[("V", k.cnt["V"])])
            k.op("gpsimd", lambda e: e.dma_start(out=dbg_m[0:32, 32:33], in_=gk2[:], allow_slow_non_contiguous=True),
                 inc=("oqA", 16))
            k.op("gpsimd", lambda e: e.dma_start(out=dbg_m[0:32, 33:34], in_=gq2[:], allow_slow_non_contiguous=True),
                 inc=("oqA", 16))
            k.op("gpsimd", lambda e: e.dma_start(out=dbg_m[0:1, 34:35], in_=bmax_sb[:]),
                 inc=("oqA", 16))
            k.op("gpsimd", lambda e: e.dma_start(out=dbg_m[0:128, 35:67], in_=Bsat[:, 0:32]),
                 inc=("oqA", 16))
        for p in range(128):
            k.op("sync", lambda e, p=p: e.dma_start(
                out=band[p:p + 1, :].rearrange("o (t j) -> o t j", t=NBAND),
                in_=dview(bexp_dram, 127 - p, [[0, 1], [640, NBAND], [1, 512]])),
                inc=("sq", 16), wait=[("bx", relprep_gq)] if p == 0 else None)
        band_sq = k.cnt["sq"]
        bias_V = k.cnt["V"]
        scD.close()

        scE = contextlib.ExitStack()
        KP = scE.enter_context(sb("KP", [128, S], BF16))
        QTs = scE.enter_context(sb("QTs", [128, T], BF16))
        VP = scE.enter_context(sb("VP", [128, KCN * 130], BF16))
        EA = scE.enter_context(sb("EA", [128, 1024], BF16))
        EB = scE.enter_context(sb("EB", [128, 1024], BF16))
        cTA = scE.enter_context(sb("cTA", [65, TH], F32))
        cTB = scE.enter_context(sb("cTB", [65, TH], F32))
        ctn = scE.enter_context(sb("ctn", [128, 65], F32))
        rdc = scE.enter_context(sb("rdc", [128, 1], F32))
        ctxn = [scE.enter_context(sb(f"ctxn{i}", [128, 64], BF16)) for i in range(2)]

        vone = k.op("vector", lambda e: e.memset(VP[:], 1.0), inc=("V", 1),
                    wait=[("T", k.cnt["T"])] + vrot_bar + [("cc", cc_kt)])

        seA_h = {}
        seB_h = {}
        pvA_h = {}
        pvB_h = {}
        prev_evac = 0
        ctx_dma = [0, 0]

        def emit_pv(pair, qt, kc, wA, wB):
            pvA_h[kc] = k.op(
                "tensor", lambda e, kc=kc:
                e.matmul(ps_c[0:65, 0:TH], VP[:, kc * 130: kc * 130 + 65],
                         EA[:, (kc % 2) * 512: (kc % 2) * 512 + TH],
                         start=(kc == 0), stop=(kc == KCN - 1)),
                inc=("T", 1),
                wait=wA + ([("V", prev_evac)] if kc == 0 else []))
            pvB_h[kc] = k.op(
                "tensor", lambda e, kc=kc:
                e.matmul(ps_c[0:65, 512:512 + TH], VP[:, kc * 130 + 65: kc * 130 + 130],
                         EB[:, (kc % 2) * 512: (kc % 2) * 512 + TH],
                         start=(kc == 0), stop=(kc == KCN - 1)),
                inc=("T", 1), wait=wB)

        for pair in range(PAIRS):
            kp_t = 0
            for u in range(8):
                kp_t = k.op("gpsimd", lambda e, u=u, pair=pair:
                            e.indirect_dma_start(
                                KP[:, u * 1024:(u + 1) * 1024], None, kt_out[:, :],
                                IndirectOffsetOnAxis(
                                    idxs[:, u * 16 + pair:u * 16 + pair + 1], 0)),
                            inc=("kp", 16),
                            wait=([("cc", cc_kt), ("idx", ld_idx)] + vrot_bar
                                  if pair == 0 and u == 0 else
                                  ([("T", k.cnt["T"])] if u == 0 else None)))
            vp_t = k.op("sync", lambda e, pair=pair: e.dma_start(
                out=fview(VP, 0, [[130, KCN], [1, 64]]),
                in_=dview(v_rot, 128 * pair,
                          [[D, 128], [128 * D, KCN], [1, 64]])),
                inc=("vp", 16),
                wait=(vrot_bar + [("V", vone), ("sq", band_sq)] if pair == 0
                      else [("T", k.cnt["T"])]))
            vp_t = k.op("sync", lambda e, pair=pair: e.dma_start(
                out=fview(VP, 65, [[130, KCN], [1, 64]]),
                in_=dview(v_rot, 128 * pair + 64,
                          [[D, 128], [128 * D, KCN], [1, 64]])),
                inc=("vp", 16))
            vp_t = k.op("sync", lambda e, pair=pair: e.dma_start(
                out=QTs[:], in_=q_dram[pair * 128:(pair + 1) * 128, :]),
                inc=("vp", 16), wait=(qkv_bar + vrot_bar) if pair == 0 else None)
            for qt in range(2):
                pend = None
                for kc in range(KCN):
                    h = kc % 2
                    hA, hB = 2 * pair, 2 * pair + 1
                    is_mult = (qt, kc) in MIDX
                    m = MIDX.get((qt, kc))
                    kpr = kc // 2
                    w0 = [("S", seA_h.get(kc - 2, 0))]
                    if kc == 0 and qt == 0:
                        w0 += [("kp", kp_t), ("vp", vp_t), ("V", bias_V)]
                        if pair == 0:
                            w0 += [("S", k.cnt["S"])]
                    tsA = k.op("tensor", lambda e, kc=kc, h=h, qt=qt:
                               e.matmul(psA(h), KP[0:64, kc * 128:(kc + 1) * 128],
                                        QTs[0:64, qt * TH:(qt + 1) * TH],
                                        start=True, stop=True, tile_position=(0, 0)),
                               inc=("T", 1), wait=w0)
                    tsB = k.op("tensor", lambda e, kc=kc, h=h, qt=qt:
                               e.matmul(psB(h), KP[64:128, kc * 128:(kc + 1) * 128],
                                        QTs[64:128, qt * TH:(qt + 1) * TH],
                                        start=True, stop=True, tile_position=(64, 0)),
                               inc=("T", 1), wait=[("S", seB_h.get(kc - 2, 0))])
                    if pend is not None:
                        emit_pv(pair, qt, *pend)
                    if is_mult:
                        biasA = Mneg[:, hA:hA + 1]
                        biasB = Mneg[:, hB:hB + 1]
                    else:
                        biasA = Bsat[:, hA * 64 + qt * 32 + kpr: hA * 64 + qt * 32 + kpr + 1]
                        biasB = Bsat[:, hB * 64 + qt * 32 + kpr: hB * 64 + qt * 32 + kpr + 1]
                    seA = k.op("scalar", lambda e, h=h, biasA=biasA:
                               e.activation(out=EA[:, h * 512: h * 512 + TH],
                                            in_=psA(h), func=AF.Exp,
                                            bias=biasA, scale=0.125),
                               inc=("S", 1),
                               wait=[("T", tsA), ("T", pvA_h.get(kc - 2, 0))])
                    seA_h[kc] = seA
                    seB = k.op("scalar", lambda e, h=h, biasB=biasB:
                               e.activation(out=EB[:, h * 512: h * 512 + TH],
                                            in_=psB(h), func=AF.Exp,
                                            bias=biasB, scale=0.125),
                               inc=("S", 1),
                               wait=[("T", tsB), ("T", pvB_h.get(kc - 2, 0))])
                    seB_h[kc] = seB
                    wA = [("S", seA)]
                    wB = [("S", seB)]
                    if is_mult:
                        bvA = k.op("vector", lambda e, h=h, m=m:
                                   e.tensor_tensor(EA[:, h * 512: h * 512 + TH],
                                                   EA[:, h * 512: h * 512 + TH],
                                                   band[:, m * 512:(m + 1) * 512],
                                                   ALU.mult),
                                   inc=("V", 1), wait=[("S", seA)])
                        bvB = k.op("vector", lambda e, h=h, m=m:
                                   e.tensor_tensor(EB[:, h * 512: h * 512 + TH],
                                                   EB[:, h * 512: h * 512 + TH],
                                                   band[:, m * 512:(m + 1) * 512],
                                                   ALU.mult),
                                   inc=("V", 1), wait=[("S", seB)])
                        wA = [("V", bvA)]
                        wB = [("V", bvB)]
                    pend = (kc, wA, wB)
                emit_pv(pair, qt, *pend)
                evA = k.op("vector", lambda e: e.tensor_copy(cTA[:], ps_c[0:65, 0:TH]),
                           inc=("V", 1), wait=[("T", pvA_h[KCN - 1])])
                evB = k.op("vector", lambda e: e.tensor_copy(cTB[:], ps_c[0:65, 512:512 + TH]),
                           inc=("V", 1), wait=[("T", pvB_h[KCN - 1])])
                prev_evac = evB
                prev_vs = 0
                for hh in range(2):
                    src_t = cTA if hh == 0 else cTB
                    evw = evA if hh == 0 else evB
                    for tq in range(4):
                        h2 = tq % 2
                        ci = tq % 2
                        cxs = "cxA" if ci == 0 else "cxB"
                        tv = k.op("tensor", lambda e, src_t=src_t, tq=tq, h2=h2:
                                  e.transpose(ps_d[:, h2 * 512: h2 * 512 + 65],
                                              src_t[:, tq * 128:(tq + 1) * 128],
                                              idf[0:65, 0:65]),
                                  inc=("T", 1), wait=[("V", evw), ("V", k.cnt["V"])])
                        vc = k.op("vector", lambda e, h2=h2:
                                  e.tensor_copy(ctn[:], ps_d[:, h2 * 512: h2 * 512 + 65]),
                                  inc=("V", 1), wait=[("T", tv), ("V", prev_vs)])
                        vr2 = k.op("vector", lambda e:
                                   e.reciprocal(rdc[:], ctn[:, 64:65]),
                                   inc=("V", 1), wait=[("V", vc)])
                        vs = k.op("vector", lambda e, ci=ci:
                                  e.tensor_scalar(ctxn[ci][:], ctn[:, 0:64], rdc[:],
                                                  None, ALU.mult),
                                  inc=("V", 1),
                                  wait=[("V", vr2), (cxs, ctx_dma[ci])])
                        prev_vs = vs
                        ctx_dma[ci] = k.op(
                            "gpsimd",
                            lambda e, pair=pair, qt=qt, hh=hh, tq=tq, ci=ci:
                            e.dma_start(
                                out=ctx_dram[qt * TH + tq * 128: qt * TH + (tq + 1) * 128,
                                             pair * 128 + hh * 64: pair * 128 + hh * 64 + 64],
                                in_=ctxn[ci][:]),
                            inc=(cxs, 16), wait=[("V", vs)])
        attn_bar = [("cxA", k.cnt["cxA"]), ("cxB", k.cnt["cxB"])]
        scE.close()

        scF = contextlib.ExitStack()
        ctile = [scF.enter_context(sb(f"ctile{i}", [128, D], BF16)) for i in range(2)]
        ctxT = [scF.enter_context(sb(f"ctxT{i}", [128, T], BF16)) for i in range(DC)]
        wstrF = [scF.enter_context(sb(f"wsf{i}", [128, 2048], BF16)) for i in range(2)]
        ct_t = [k.cnt["T"], k.cnt["T"]]
        f_entry_V = k.cnt["V"]
        for tt in range(8):
            slot = tt % 2
            cts = "ctA" if slot == 0 else "ctB"
            lc = k.op("sync", lambda e, tt=tt, slot=slot:
                      e.dma_start(out=ctile[slot][:],
                                  in_=ctx_dram[tt * 128:(tt + 1) * 128, :]),
                      inc=(cts, 16),
                      wait=attn_bar + [("T", ct_t[slot])])
            for dc in range(DC):
                h2 = dc % 2
                tv = k.op("tensor", lambda e, slot=slot, dc=dc, h2=h2:
                          e.transpose(psC(h2, 64).bitcast(BF16),
                                      ctile[slot][:, dc * 128:(dc + 1) * 128], idb[:]),
                          inc=("T", 1),
                          wait=[(cts, lc), ("V", k.cnt["V"])]
                               + ([("V", f_entry_V)] if tt == 0 and dc == 0 else []))
                k.op("vector", lambda e, dc=dc, tt=tt, h2=h2:
                     e.tensor_copy(ctxT[dc][:, tt * 128:(tt + 1) * 128],
                                   psC(h2, 64).bitcast(BF16)),
                     inc=("V", 1), wait=[("T", tv)])
            ct_t[slot] = k.cnt["T"]
        ctxT_V = k.cnt["V"]
        strip_t = {"wsA": ct_t[0], "wsB": ct_t[1]}
        for fc in range(ECN):
            slot = fc % 2
            ssem = "wsA" if slot == 0 else "wsB"
            r, l = fc // 2, fc % 2
            row = (r * 4 + 3) * 256 + l * 128
            lw = k.op("sync", lambda e, row=row, slot=slot:
                      e.dma_start(out=wstrF[slot][:], in_=pa_all[row:row + 128, :]),
                      inc=(ssem, 16),
                      wait=[("T", strip_t[ssem])] + (attn_bar if fc < 2 else []))
            for th in range(2):
                h = (fc * 2 + th) % 2
                for ec in range(DC):
                    w0 = ([(ssem, lw), ("V", ctxT_V), ("V", k.cnt["V"])]
                          if ec == 0 else None)
                    k.op("tensor", lambda e, slot=slot, ec=ec, h=h, th=th:
                         e.matmul(psA(h), wstrF[slot][:, ec * 128:(ec + 1) * 128],
                                  ctxT[ec][:, th * TH:(th + 1) * TH],
                                  start=(ec == 0), stop=(ec == DC - 1)),
                         inc=("T", 1) if ec == DC - 1 else None, wait=w0)
                to = k.cnt["T"]
                k.op("vector", lambda e, fc=fc, th=th, h=h:
                     e.tensor_add(XT[fc][:, th * TH:(th + 1) * TH], psA(h),
                                  XT[fc][:, th * TH:(th + 1) * TH]),
                     inc=("V", 1), wait=[("T", to)])
            strip_t[ssem] = k.cnt["T"]
        scF.close()
        sc2.close()

    if STAGE >= 3:
        sc_f2 = contextlib.ExitStack()
        wstr2 = [sc_f2.enter_context(sb(f"w2s{i}", [128, H], BF16)) for i in range(2)]
        Z2 = sc_f2.enter_context(sb("Z2", [128, HC * TH], BF16))
        SG2 = sc_f2.enter_context(sb("SG2", [128, TH], BF16))
        emit_ffn(False, wstr2, Z2, SG2)
        sc_f2.close()

    ostage = [stack.enter_context(sb(f"os{i}", [128, D], F32)) for i in range(2)]
    out_dma = [0, 0]
    for tt in range(8):
        slot = tt % 2
        oqs = "oqA" if slot == 0 else "oqB"
        for dc in range(DC):
            h2 = dc % 2
            tv = k.op("tensor", lambda e, tt=tt, dc=dc, h2=h2:
                      e.transpose(psC(h2, 128), XT[dc][:, tt * 128:(tt + 1) * 128],
                                  idf[:]),
                      inc=("T", 1),
                      wait=[("V", k.cnt["V"]), (oqs, out_dma[slot])])
            k.op("vector", lambda e, slot=slot, dc=dc, h2=h2:
                 e.tensor_copy(ostage[slot][:, dc * 128:(dc + 1) * 128], psC(h2, 128)),
                 inc=("V", 1), wait=[("T", tv)])
        sv = k.cnt["V"]
        out_dma[slot] = k.op("gpsimd", lambda e, tt=tt, slot=slot:
                             e.dma_start(out=y_out[tt * 128:(tt + 1) * 128, :],
                                         in_=ostage[slot][:]),
                             inc=(oqs, 16), wait=[("V", sv)])
    if DEBUG and STAGE >= 2:
        for (dst, srct) in ((dbg_ctx, ctx_dram), (dbg_q, q_dram), (dbg_kt, kt_out),
                            (dbg_vr, v_rot), (dbg_kn, kn_out), (dbg_vin, v_in)):
            k.op("gpsimd", lambda e, dst=dst, srct=srct:
                 e.dma_start(out=dst.ap(), in_=srct.ap()), inc=("oqA", 16))
        k.op("gpsimd", lambda e:
             e.dma_start(out=dbg_bx.ap(), in_=bexp_dram.ap()), inc=("oqA", 16))
    k.wait("gpsimd", "oqA", k.cnt["oqA"])
    k.wait("gpsimd", "oqB", k.cnt["oqB"])

    with nc.Block() as block:
        @block.sync
        def _(e):
            for f in k.st["sync"]:
                f(e)

        @block.gpsimd
        def _(e):
            for f in k.st["gpsimd"]:
                f(e)

        @block.vector
        def _(e):
            for f in k.st["vector"]:
                f(e)

        @block.scalar
        def _(e):
            for f in k.st["scalar"]:
                f(e)

        @block.tensor
        def _(e):
            for f in k.st["tensor"]:
                f(e)

    stack.close()
    return nc


def kernel(L0_attn_hidden_states, L0_attn_wq, L0_attn_wk, L0_attn_wv,
           L0_attn_wo, L0_attn_rel_pos, L0_ffn_wq_gate, L0_ffn_s_gate,
           L0_ffn_wq_up, L0_ffn_s_up, L0_ffn_wq_down, L0_ffn_s_down):
    x = np.ascontiguousarray(np.asarray(L0_attn_hidden_states, dtype=np.float32))
    wq = np.asarray(L0_attn_wq, np.float32)
    wk = np.asarray(L0_attn_wk, np.float32)
    wv = np.asarray(L0_attn_wv, np.float32)
    wo = np.asarray(L0_attn_wo, np.float32)
    rel = np.asarray(L0_attn_rel_pos, np.float32)
    wg = np.asarray(L0_ffn_wq_gate, np.int32)
    sg = np.asarray(L0_ffn_s_gate, np.float32)
    wu = np.asarray(L0_ffn_wq_up, np.int32)
    su = np.asarray(L0_ffn_s_up, np.float32)
    wd = np.asarray(L0_ffn_wq_down, np.int32)
    sd = np.asarray(L0_ffn_s_down, np.float32)

    identf = np.eye(128, dtype=np.float32)
    identb = np.eye(128, dtype=ml_dtypes.bfloat16)

    def pad_rows(a, n):
        if a.shape[0] == n:
            return a
        out = np.zeros((n,) + a.shape[1:], a.dtype)
        out[:a.shape[0]] = a
        return out

    in_maps = []
    for c in range(NC):
        g0, g1 = 768 * c, min(768 * (c + 1), H)
        in_maps.append({
            "x_in": np.ascontiguousarray(x[0, T * c:T * (c + 1), :]),
            "wg_sl": pad_rows(np.ascontiguousarray(wg[g0:g1]), 768),
            "wu_sl": pad_rows(np.ascontiguousarray(wu[g0:g1]), 768),
            "wd_sl": np.ascontiguousarray(wd[256 * c:256 * (c + 1)]),
            "sg_sl": np.ascontiguousarray(pad_rows(sg[g0:g1], 768).reshape(HCP, 128).T),
            "su_sl": np.ascontiguousarray(pad_rows(su[g0:g1], 768).reshape(HCP, 128).T),
            "sd_sl": np.ascontiguousarray(sd[256 * c:256 * (c + 1)].reshape(2, 128).T),
            "wa_sl": np.ascontiguousarray(np.concatenate(
                [w[256 * c:256 * (c + 1)] for w in (wq, wk, wv, wo)], axis=0)),
            "rel_tabt": _host_rel_table(rel, c).astype(np.float32),
            "idx_tab": _host_idx_tables(c),
            "ident_f": identf,
            "ident_b": identb,
        })

    nc = build_nc()
    res = run_bass_kernel_spmd(nc, in_maps, core_ids=list(range(NC)))
    if DEBUG:
        DEBUG_OUT.clear()
        for key in res.results[0]:
            DEBUG_OUT[key] = [np.asarray(res.results[c][key]) for c in range(NC)]
    y = np.concatenate([res.results[c]["y"] for c in range(NC)], axis=0)
    return y[None, :, :].astype(np.float32)


# version 04
# speedup vs baseline: 1.2465x; 1.0420x over previous
import contextlib
import numpy as np
import ml_dtypes
import bass_rust
import concourse.bass as bass
import concourse.mybir as mybir
from concourse.bass import IndirectOffsetOnAxis
from concourse.bass_utils import run_bass_kernel_spmd

F32 = mybir.dt.float32
BF16 = mybir.dt.bfloat16
I32 = mybir.dt.int32
AF = mybir.ActivationFunctionType
ALU = mybir.AluOpType
AX = mybir.AxisListType

NC = 8
T = 1024
D = 2048
H = 5632
S = 8192
DC = 16
HC = 44
KCN = 64
ECN = 16
HEADS = 32
PAIRS = 16
TH = 512
HCP = 6
EPS = 1e-5
HW = 2816
HWC = 22

STAGE = 3
DEBUG = False
DEBUG_OUT = {}


def _dprime(c, qt, kc_rel):
    kc_abs = (kc_rel + 8 * c) % 64
    return 128 * kc_abs - (1024 * c + 512 * qt) + 511


def _geometry():
    mult = []
    for qt in range(2):
        for kc in range(64):
            if any(-127 < _dprime(c, qt, kc) < 1533 for c in range(8)):
                mult.append((qt, kc))
    return mult, {t: i for i, t in enumerate(mult)}


MULT, MIDX = _geometry()
NBAND = len(MULT)
NP_P = 1024
NP_S = 128
NTAB = NP_P + NP_S + NBAND * 640
BAND0 = NP_P + NP_S


def _host_rel_table(rel_pos, c):
    idx = np.zeros(NTAB, dtype=np.int64)
    idx[:NP_P] = np.minimum(np.arange(NP_P), 1022)
    for qt in range(2):
        for kc in range(64):
            d = _dprime(c, qt, kc)
            idx[NP_P + qt * 64 + kc] = 0 if d <= -127 else 1022
    for m, (qt, kc) in enumerate(MULT):
        d = _dprime(c, qt, kc)
        w = np.arange(640)
        idx[BAND0 + 640 * m: BAND0 + 640 * (m + 1)] = np.clip(d + 127 - w, 0, 1022)
    return np.ascontiguousarray(rel_pos[idx].T)


def _host_idx_tables(c):
    p = np.arange(128)
    idx_k = np.zeros((128, 128), np.int32)
    for u in range(8):
        rot = (u + c) % 8
        for j in range(PAIRS):
            idx_k[:, u * 16 + j] = rot * 2048 + 128 * j + p
    idx_v = np.zeros((128, 64), np.int32)
    for kc in range(64):
        kc_abs = (kc + 8 * c) % 64
        idx_v[:, kc] = kc_abs * 128 + p
    return np.concatenate([idx_k, idx_v], axis=1)


def fview(t, off, dims):
    a = t[:]
    return bass_rust.AP(a.tensor, a.offset + off, [list(a.ap[0])] + dims)


def dview(t, off, dims):
    a = t.ap()
    return bass_rust.AP(a.tensor, off, dims)


class KB:
    ENGS = ("sync", "gpsimd", "vector", "scalar", "tensor")

    def __init__(self, nc):
        self.nc = nc
        self.st = {e: [] for e in self.ENGS}
        self.sem = {}
        self.cnt = {}

    def new_sem(self, stack, name):
        self.sem[name] = stack.enter_context(self.nc.semaphore(name))
        self.cnt[name] = 0

    def op(self, eng, f, inc=None, wait=None):
        tgt = None
        if inc is not None:
            self.cnt[inc[0]] += inc[1]
            tgt = self.cnt[inc[0]]
        waits = [(n, t) for (n, t) in (wait or []) if t > 0]
        sems = self.sem

        def g(e, f=f, waits=waits, inc=inc):
            for wn, wt in waits:
                e.wait_ge(sems[wn], wt)
            ins = f(e)
            if inc is not None and ins is not None:
                ins.then_inc(sems[inc[0]], inc[1])
        self.st[eng].append(g)
        return tgt

    def wait(self, eng, name, tgt):
        sems = self.sem
        if tgt > 0:
            self.st[eng].append(lambda e, n=name, t=tgt: e.wait_ge(sems[n], t))


def build_nc():
    nc = bass.Bass("TRN2", target_bir_lowering=False)
    stack = contextlib.ExitStack()

    x_in = nc.dram_tensor("x_in", [T, D], F32, kind="ExternalInput")
    wg_sl = nc.dram_tensor("wg_sl", [HCP * 128, D], I32, kind="ExternalInput")
    wu_sl = nc.dram_tensor("wu_sl", [HCP * 128, D], I32, kind="ExternalInput")
    wd_sl = nc.dram_tensor("wd_sl", [256, H], I32, kind="ExternalInput")
    sg_sl = nc.dram_tensor("sg_sl", [128, HCP], F32, kind="ExternalInput")
    su_sl = nc.dram_tensor("su_sl", [128, HCP], F32, kind="ExternalInput")
    sd_sl = nc.dram_tensor("sd_sl", [128, 2], F32, kind="ExternalInput")
    wa_sl = nc.dram_tensor("wa_sl", [1024, D], F32, kind="ExternalInput")
    rel_tabt = nc.dram_tensor("rel_tabt", [64, NTAB], F32, kind="ExternalInput")
    idx_tab = nc.dram_tensor("idx_tab", [128, 192], I32, kind="ExternalInput")
    ident_f = nc.dram_tensor("ident_f", [128, 128], F32, kind="ExternalInput")
    ident_b = nc.dram_tensor("ident_b", [128, 128], BF16, kind="ExternalInput")
    y_out = nc.dram_tensor("y", [T, D], F32, kind="ExternalOutput")
    if DEBUG:
        dbg_ctx = nc.dram_tensor("dbg_ctx", [T, D], BF16, kind="ExternalOutput")
        dbg_q = nc.dram_tensor("dbg_q", [D, T], BF16, kind="ExternalOutput")
        dbg_kt = nc.dram_tensor("dbg_kt", [NC * D, T], BF16, kind="ExternalOutput")
        dbg_vr = nc.dram_tensor("dbg_vr", [S, D], BF16, kind="ExternalOutput")
        dbg_vin = nc.dram_tensor("dbg_vin", [T, D], BF16, kind="ExternalOutput")
        dbg_acc = nc.dram_tensor("dbg_acc", [128, 520], F32, kind="ExternalOutput")
        dbg_vp = nc.dram_tensor("dbg_vp", [128, KCN * 130], BF16, kind="ExternalOutput")
        dbg_m = nc.dram_tensor("dbg_m", [128, 96], F32, kind="ExternalOutput")
        dbg_kn = nc.dram_tensor("dbg_kn", [NC * HEADS, T], F32, kind="ExternalOutput")
        dbg_bx = nc.dram_tensor("dbg_bx", [NBAND * 640 + 64], BF16, kind="ExternalOutput")

    pgu_out = nc.dram_tensor("pgu_out", [2 * HCP * 128, D], BF16)
    pgu_all = nc.dram_tensor("pgu_all", [NC * 2 * HCP * 128, D], BF16)
    pd_out = nc.dram_tensor("pd_out", [256, H], BF16)
    pd_all = nc.dram_tensor("pd_all", [2048, H], BF16)
    pa_out = nc.dram_tensor("pa_out", [1024, D], BF16)
    pa_all = nc.dram_tensor("pa_all", [8192, D], BF16)
    kt_in = nc.dram_tensor("kt_in", [D, T], BF16)
    kt_out = nc.dram_tensor("kt_out", [NC * D, T], BF16)
    v_in = nc.dram_tensor("v_in", [T, D], BF16)
    v_out = nc.dram_tensor("v_out", [S, D], BF16)
    v_rot = nc.dram_tensor("v_rot", [S, D], BF16)
    kn_in = nc.dram_tensor("kn_in", [HEADS, T], F32)
    kn_out = nc.dram_tensor("kn_out", [NC * HEADS, T], F32)
    qn_dram = nc.dram_tensor("qn_dram", [HEADS, T], F32)
    q_dram = nc.dram_tensor("q_dram", [D, T], BF16)
    bexp_dram = nc.dram_tensor("bexp_dram", [NBAND * 640 + 64], BF16)
    ctx_dram = nc.dram_tensor("ctx_dram", [T, D], BF16)

    k = KB(nc)
    for s in ("sq", "cc", "T", "V", "S", "wsA", "wsB", "kp", "vp", "kp2", "vp2",
              "icl", "msc", "xA", "xB", "pA", "pB", "psA2", "psB2", "rl", "bx",
              "ksA", "ksB", "vtA", "vtB", "nst", "vrA", "vrB", "vrC", "vrD", "gkb",
              "ctA", "ctB", "cxA", "cxB", "oqA", "oqB", "idx"):
        k.new_sem(stack, s)

    sb = nc.sbuf_tensor
    XT = [stack.enter_context(sb(f"xt{i}", [128, T], F32)) for i in range(DC)]
    LN1 = [stack.enter_context(sb(f"ln{i}", [128, TH], BF16)) for i in range(DC)]
    idf = stack.enter_context(sb("idf", [128, 128], F32))
    idb = stack.enter_context(sb("idb", [128, 128], BF16))
    ones_col_b = stack.enter_context(sb("ones_col_b", [128, 1], BF16))
    w64_col = stack.enter_context(sb("w64_col", [128, 1], BF16))
    mask_top = stack.enter_context(sb("mask_top", [128, 1], BF16))
    mask_bot = stack.enter_context(sb("mask_bot", [128, 1], BF16))
    ones_row_b = stack.enter_context(sb("ones_row_b", [1, 128], BF16))
    ones_row_f = stack.enter_context(sb("ones_row_f", [1, 128], F32))
    brow_ps = stack.enter_context(sb("brow_ps", [1, NP_P + NP_S], F32))
    bmax_sb = stack.enter_context(sb("bmax_sb", [1, 1], F32))
    rowA = stack.enter_context(sb("rowA", [1, TH], F32))
    rowB = stack.enter_context(sb("rowB", [1, TH], F32))
    rowC = stack.enter_context(sb("rowC", [1, TH], F32))
    rowBF = stack.enter_context(sb("rowBF", [1, TH], BF16))
    rowBF2 = stack.enter_context(sb("rowBF2", [1, TH], BF16))
    xbt = stack.enter_context(sb("xbt", [128, TH], BF16))
    xsq = stack.enter_context(sb("xsq", [128, TH], BF16))
    tmpf = stack.enter_context(sb("tmpf", [128, TH], F32))
    Abc = stack.enter_context(sb("Abc", [128, TH], BF16))
    Bbc = stack.enter_context(sb("Bbc", [128, TH], BF16))

    ps_a = stack.enter_context(nc.psum_tensor([128, 1024], F32))
    ps_b = stack.enter_context(nc.psum_tensor([128, 1024], F32))
    ps_c = stack.enter_context(nc.psum_tensor([128, 1024], F32))
    ps_d = stack.enter_context(nc.psum_tensor([128, 1024], F32))

    def psA(h):
        return ps_a[:, h * 512: h * 512 + TH]

    def psB(h):
        return ps_b[:, h * 512: h * 512 + TH]

    def psC(h, w=TH):
        return ps_c[:, h * 512: h * 512 + w]

    def psD(h, w=TH):
        return ps_d[:, h * 512: h * 512 + w]

    k.op("sync", lambda e: e.dma_start(out=idf[:], in_=ident_f[:, :]), inc=("icl", 16))
    k.op("sync", lambda e: e.dma_start(out=idb[:], in_=ident_b[:, :]), inc=("icl", 16))
    id_sq = k.cnt["icl"]
    for buf, val in ((ones_col_b, 1.0), (w64_col, 1.0 / 64), (ones_row_b, 1.0),
                     (ones_row_f, 1.0)):
        k.op("vector", lambda e, b=buf, v=val: e.memset(b[:], v), inc=("V", 1))
    k.op("vector", lambda e: e.memset(mask_top[:], 0.0), inc=("V", 1))
    k.op("vector", lambda e: e.memset(mask_top[0:64, :], 1.0), inc=("V", 1))
    k.op("vector", lambda e: e.memset(mask_bot[:], 0.0), inc=("V", 1))
    k.op("vector", lambda e: e.memset(mask_bot[64:128, :], 1.0), inc=("V", 1))
    consts_V = k.cnt["V"]

    sc1 = contextlib.ExitStack()
    xstage = [sc1.enter_context(sb(f"xs{i}", [128, D], F32)) for i in range(2)]
    pin = [sc1.enter_context(sb(f"pin{i}", [128, HW], I32)) for i in range(2)]
    pdq = [sc1.enter_context(sb(f"pdq{i}", [128, HW], BF16)) for i in range(2)]
    pstr = [sc1.enter_context(sb(f"pstr{i}", [128, HW], BF16)) for i in range(2)]
    scg = sc1.enter_context(sb("scg", [128, HCP], F32))
    scu = sc1.enter_context(sb("scu", [128, HCP], F32))
    scd = sc1.enter_context(sb("scd", [128, 2], F32))
    relc = sc1.enter_context(sb("relc", [64, 2048], F32))
    relb = sc1.enter_context(sb("relb", [64, 2048], BF16))
    bchunk = sc1.enter_context(sb("bchunk", [1, 2048], F32))
    bexpc = sc1.enter_context(sb("bexpc", [1, 2048], BF16))

    ev_hist = []
    for tt in range(8):
        slot = tt % 2
        xsem = "xA" if slot == 0 else "xB"
        lw = k.op("sync", lambda e, tt=tt, slot=slot:
                  e.dma_start(out=xstage[slot][:], in_=x_in[tt * 128:(tt + 1) * 128, :]),
                  inc=(xsem, 16),
                  wait=[("icl", id_sq)] + ([("V", ev_hist[tt - 2])] if tt >= 2 else []))
        for dc in range(DC):
            h = dc % 2
            tT = k.op("tensor", lambda e, slot=slot, dc=dc, h=h:
                      e.transpose(psC(h, 128), xstage[slot][:, dc * 128:(dc + 1) * 128],
                                  idf[:]),
                      inc=("T", 1),
                      wait=[(xsem, lw)] + ([("V", k.cnt["V"])] if (tt or dc) else []))
            k.op("vector", lambda e, dc=dc, tt=tt, h=h:
                 e.tensor_copy(XT[dc][:, tt * 128:(tt + 1) * 128], psC(h, 128)),
                 inc=("V", 1), wait=[("T", tT)])
        ev_hist.append(k.cnt["V"])

    k.op("sync", lambda e: e.dma_start(out=scg[:], in_=sg_sl[:, :]), inc=("msc", 16))
    k.op("sync", lambda e: e.dma_start(out=scu[:], in_=su_sl[:, :]), inc=("msc", 16))
    k.op("sync", lambda e: e.dma_start(out=scd[:], in_=sd_sl[:, :]), inc=("msc", 16))
    sc_sq = k.cnt["msc"]
    k.op("vector", lambda e: e.tensor_scalar_mul(scd[:], scd[:], 0.5), inc=("V", 1),
         wait=[("msc", sc_sq)])
    prep_sc_V = k.cnt["V"]

    prep_dq = [0, 0]
    prep_st = [0, 0]

    def prep_chunk(load_fn, width, dcw, dequant, sc, sc_col, store_fn, li):
        slot = li % 2
        psem = "pA" if slot == 0 else "pB"
        ssem = "psA2" if slot == 0 else "psB2"
        lw = k.op("sync", lambda e, slot=slot: load_fn(e, slot),
                  inc=(psem, 16), wait=[("V", prep_dq[slot])])
        if dequant:
            dqv = k.op("vector", lambda e, slot=slot:
                       e.tensor_scalar(pdq[slot][:, 0:width], pin[slot][:, 0:width],
                                       -128.0, sc[:, sc_col:sc_col + 1],
                                       ALU.add, ALU.mult),
                       inc=("V", 1), wait=[(psem, lw), ("V", prep_sc_V)])
        else:
            dqv = k.op("vector", lambda e, slot=slot:
                       e.tensor_copy(pdq[slot][:, 0:width],
                                     pin[slot].bitcast(F32)[:, 0:width]),
                       inc=("V", 1), wait=[(psem, lw)])
        prep_dq[slot] = dqv
        for dc in range(dcw):
            h = dc % 2
            tT = k.op("tensor", lambda e, slot=slot, dc=dc, h=h:
                      e.transpose(psD(h, 64).bitcast(BF16),
                                  pdq[slot][:, dc * 128:(dc + 1) * 128], idb[:]),
                      inc=("T", 1), wait=[("V", dqv), ("V", k.cnt["V"])])
            k.op("vector", lambda e, slot=slot, dc=dc, h=h, ssem=ssem:
                 e.tensor_copy(pstr[slot][:, dc * 128:(dc + 1) * 128],
                               psD(h, 64).bitcast(BF16)),
                 inc=("V", 1), wait=[("T", tT), (ssem, prep_st[slot])])
        sv = k.cnt["V"]
        prep_st[slot] = k.op("gpsimd", lambda e, slot=slot: store_fn(e, slot),
                             inc=(ssem, 16), wait=[("V", sv)])

    li = 0
    for l in range(HCP):
        prep_chunk(lambda e, slot, l=l: e.dma_start(
            out=pin[slot][:, 0:2048], in_=wg_sl[l * 128:(l + 1) * 128, :]),
            2048, 16, True, scg, l,
            lambda e, slot, l=l: e.dma_start(
                out=pgu_out[l * 128:(l + 1) * 128, :], in_=pstr[slot][:, 0:2048]), li)
        li += 1
    for l in range(HCP):
        prep_chunk(lambda e, slot, l=l: e.dma_start(
            out=pin[slot][:, 0:2048], in_=wu_sl[l * 128:(l + 1) * 128, :]),
            2048, 16, True, scu, l,
            lambda e, slot, l=l: e.dma_start(
                out=pgu_out[(HCP + l) * 128:(HCP + l + 1) * 128, :],
                in_=pstr[slot][:, 0:2048]), li)
        li += 1
    gu_bar = [("psA2", k.cnt["psA2"]), ("psB2", k.cnt["psB2"])]
    cc_gu = k.op("gpsimd", lambda e: e.collective_compute(
        "AllGather", ALU.bypass, replica_groups=[list(range(NC))],
        ins=[pgu_out.ap().opt()], outs=[pgu_all.ap().opt()]),
        inc=("cc", 1), wait=gu_bar)
    for l in range(2):
        for hh in range(2):
            prep_chunk(lambda e, slot, l=l, hh=hh: e.dma_start(
                out=pin[slot][:, 0:HW],
                in_=wd_sl[l * 128:(l + 1) * 128, hh * HW:(hh + 1) * HW]),
                HW, HWC, True, scd, l,
                lambda e, slot, l=l, hh=hh: e.dma_start(
                    out=pd_out[l * 128:(l + 1) * 128, hh * HW:(hh + 1) * HW],
                    in_=pstr[slot][:, 0:HW]), li)
            li += 1
    for l in range(8):
        prep_chunk(lambda e, slot, l=l: e.dma_start(
            out=pin[slot].bitcast(F32)[:, 0:2048], in_=wa_sl[l * 128:(l + 1) * 128, :]),
            2048, 16, False, None, 0,
            lambda e, slot, l=l: e.dma_start(
                out=pa_out[l * 128:(l + 1) * 128, :], in_=pstr[slot][:, 0:2048]), li)
        li += 1
    prep_bar = [("psA2", k.cnt["psA2"]), ("psB2", k.cnt["psB2"])]
    cc_d = k.op("gpsimd", lambda e: e.collective_compute(
        "AllGather", ALU.bypass, replica_groups=[list(range(NC))],
        ins=[pd_out.ap().opt()], outs=[pd_all.ap().opt()]),
        inc=("cc", 1))
    cc_a = k.op("gpsimd", lambda e: e.collective_compute(
        "AllGather", ALU.bypass, replica_groups=[list(range(NC))],
        ins=[pa_out.ap().opt()], outs=[pa_all.ap().opt()]),
        inc=("cc", 1))

    rel_cv = rel_mm = rel_exp = rel_dma = 0
    nchunks = (NTAB + 2047) // 2048
    for ci in range(nchunks):
        c0 = ci * 2048
        cw = min(2048, NTAB - c0)
        lw = k.op("sync", lambda e, c0=c0, cw=cw:
                  e.dma_start(out=relc[:, 0:cw], in_=rel_tabt[:, c0:c0 + cw]),
                  inc=("rl", 16), wait=[("V", rel_cv)])
        rel_cv = k.op("vector", lambda e, cw=cw: e.tensor_copy(relb[:, 0:cw], relc[:, 0:cw]),
                      inc=("V", 1), wait=[("rl", lw), ("V", consts_V), ("T", rel_mm)])
        for n0 in range(0, cw, 512):
            nw = min(512, cw - n0)
            h = (n0 // 512) % 2
            rel_mm = k.op("tensor", lambda e, n0=n0, nw=nw, h=h:
                          e.matmul(ps_c[0:1, h * 512: h * 512 + nw],
                                   w64_col[0:64, :], relb[0:64, n0:n0 + nw],
                                   start=True, stop=True),
                          inc=("T", 1), wait=[("V", rel_cv), ("V", k.cnt["V"])])
            k.op("vector", lambda e, n0=n0, nw=nw, h=h:
                 e.tensor_copy(bchunk[:, n0:n0 + nw], ps_c[0:1, h * 512: h * 512 + nw]),
                 inc=("V", 1), wait=[("T", rel_mm), ("S", rel_exp)])
        if c0 < NP_P + NP_S:
            w = min(cw, NP_P + NP_S - c0)
            k.op("vector", lambda e, c0=c0, w=w:
                 e.tensor_copy(brow_ps[:, c0:c0 + w], bchunk[:, 0:w]), inc=("V", 1),
                 wait=[("V", k.cnt["V"] - 1)])
        if c0 + cw > BAND0:
            b0 = max(BAND0, c0)
            rel_exp = k.op("scalar", lambda e, c0=c0, b0=b0, cw=cw:
                           e.activation(out=bexpc[:, b0 - c0:cw], in_=bchunk[:, b0 - c0:cw],
                                        func=AF.Exp),
                           inc=("S", 1), wait=[("V", k.cnt["V"]), ("bx", rel_dma)])
            rel_dma = k.op("gpsimd", lambda e, c0=c0, b0=b0, cw=cw:
                           e.dma_start(out=dview(bexp_dram, b0 - BAND0,
                                                 [[0, 1], [1, c0 + cw - b0]]),
                                       in_=bexpc[0:1, b0 - c0:cw]),
                           inc=("bx", 16), wait=[("S", rel_exp)])
    k.op("vector", lambda e: e.reduce_max(bmax_sb[:], brow_ps[:, 0:1023], AX.X,
                                          apply_absolute_value=True), inc=("V", 1),
         wait=[("V", k.cnt["V"] - 1)])
    relprep_V = k.cnt["V"]
    relprep_gq = k.cnt["bx"]
    sc1.close()

    def emit_ln(th):
        t0 = th * TH
        sq_prev = 0
        for dc in range(DC):
            cv = k.op("vector", lambda e, dc=dc, t0=t0:
                      e.tensor_copy(xbt[:], XT[dc][:, t0:t0 + TH]), inc=("V", 1),
                      wait=[("T", k.cnt["T"]), ("S", sq_prev)])
            sq_prev = k.op("scalar", lambda e:
                           e.activation(out=xsq[:], in_=xbt[:], func=AF.Square),
                           inc=("S", 1), wait=[("V", cv), ("T", k.cnt["T"])])
            k.op("tensor", lambda e, dc=dc:
                 e.matmul(ps_d[0:1, 0:TH], ones_col_b[:], xbt[:],
                          start=(dc == 0), stop=(dc == DC - 1)),
                 inc=("T", 1),
                 wait=[("V", cv)] + ([("V", k.cnt["V"])] if dc == 0 else []))
            k.op("tensor", lambda e, dc=dc:
                 e.matmul(ps_d[0:1, 512:512 + TH], ones_col_b[:], xsq[:],
                          start=(dc == 0), stop=(dc == DC - 1)),
                 inc=("T", 1), wait=[("S", sq_prev)])
        tS = k.cnt["T"]
        va1 = k.op("vector", lambda e: e.tensor_scalar_mul(rowA[:], ps_d[0:1, 0:TH], 1.0 / D),
                   inc=("V", 1), wait=[("T", tS)])
        vb1 = k.op("vector", lambda e: e.tensor_scalar_mul(rowB[:], ps_d[0:1, 512:512 + TH],
                                                           1.0 / D), inc=("V", 1))
        vc1 = k.op("vector", lambda e: e.tensor_mul(rowC[:], rowA[:], rowA[:]), inc=("V", 1),
                   wait=[("V", va1)])
        v4 = k.op("vector", lambda e: e.scalar_tensor_tensor(
            rowB[:], rowB[:], float(EPS), rowC[:], ALU.add, ALU.subtract), inc=("V", 1),
            wait=[("V", vc1)])
        s1 = k.op("scalar", lambda e: e.activation(out=rowB[:], in_=rowB[:], func=AF.Ln),
                  inc=("S", 1), wait=[("V", v4)])
        s2 = k.op("scalar", lambda e: e.activation(out=rowB[:], in_=rowB[:], func=AF.Exp,
                                                   scale=-0.5), inc=("S", 1),
                  wait=[("S", s1)])
        vbf = k.op("vector", lambda e: e.tensor_copy(rowBF[:], rowB[:]), inc=("V", 1),
                   wait=[("S", s2)])
        vc2 = k.op("vector", lambda e: e.tensor_mul(rowC[:], rowA[:], rowB[:]), inc=("V", 1))
        vc3 = k.op("vector", lambda e: e.tensor_scalar_mul(rowC[:], rowC[:], -1.0),
                   inc=("V", 1), wait=[("V", vc2)])
        v8 = k.op("vector", lambda e: e.tensor_copy(rowBF2[:], rowC[:]), inc=("V", 1),
                  wait=[("V", vc3)])
        tA = k.op("tensor", lambda e: e.matmul(ps_d[:, 0:TH], ones_row_b[:], rowBF[:],
                                               start=True, stop=True),
                  inc=("T", 1), wait=[("V", v8)])
        vA = k.op("vector", lambda e: e.tensor_copy(Abc[:], ps_d[:, 0:TH]), inc=("V", 1),
                  wait=[("T", tA)])
        tB = k.op("tensor", lambda e: e.matmul(ps_d[:, 512:512 + TH], ones_row_b[:],
                                               rowBF2[:], start=True, stop=True),
                  inc=("T", 1), wait=[("V", vA)])
        k.op("vector", lambda e: e.tensor_copy(Bbc[:], ps_d[:, 512:512 + TH]), inc=("V", 1),
             wait=[("T", tB)])
        for dc in range(DC):
            vt1 = k.op("vector", lambda e, dc=dc, t0=t0:
                       e.tensor_mul(tmpf[:], XT[dc][:, t0:t0 + TH], Abc[:]), inc=("V", 1))
            k.op("vector", lambda e, dc=dc:
                 e.tensor_add(LN1[dc][:, 0:TH], tmpf[:], Bbc[:]), inc=("V", 1),
                 wait=[("V", vt1)])
        return k.cnt["V"]

    def emit_ffn(first_app, wstr, Z, SG):
        silu_hist = {}
        z_hist = {}
        strip_t = {"wsA": k.cnt["T"], "wsB": k.cnt["T"]}
        ffn_entry_V = k.cnt["V"]
        for th in range(2):
            lnv = emit_ln(th)
            for hc in range(HC):
                slot = hc % 2
                ssem = "wsA" if slot == 0 else "wsB"
                r, l = hc // HCP, hc % HCP
                g_row = (r * 2 * HCP + l) * 128
                u_row = (r * 2 * HCP + HCP + l) * 128
                wcc = [("cc", cc_gu)] if (first_app and th == 0 and hc < 2) else []
                lg = k.op("sync", lambda e, g_row=g_row, slot=slot:
                          e.dma_start(out=wstr[slot][:, 0:2048],
                                      in_=pgu_all[g_row:g_row + 128, :]),
                          inc=(ssem, 16), wait=wcc + [("T", strip_t[ssem])])
                lu = k.op("sync", lambda e, u_row=u_row, slot=slot:
                          e.dma_start(out=wstr[slot][:, 2048:4096],
                                      in_=pgu_all[u_row:u_row + 128, :]),
                          inc=(ssem, 16))
                h = hc % 2
                for dc in range(DC):
                    w0 = None
                    if dc == 0:
                        w0 = [(ssem, lu), ("V", lnv),
                              ("S", silu_hist.get((th, hc - 2), 0))]
                        if th == 0 and hc < 2:
                            w0.append(("V", ffn_entry_V))
                    k.op("tensor", lambda e, slot=slot, dc=dc, h=h:
                         e.matmul(psA(h), wstr[slot][:, dc * 128:(dc + 1) * 128],
                                  LN1[dc][:, 0:TH],
                                  start=(dc == 0), stop=(dc == DC - 1)),
                         inc=("T", 1) if dc == DC - 1 else None, wait=w0)
                tg = k.cnt["T"]
                for dc in range(DC):
                    w0 = ([(ssem, lu), ("V", z_hist.get((th, hc - 2), 0))]
                          if dc == 0 else None)
                    k.op("tensor", lambda e, slot=slot, dc=dc, h=h:
                         e.matmul(psB(h),
                                  wstr[slot][:, 2048 + dc * 128: 2048 + (dc + 1) * 128],
                                  LN1[dc][:, 0:TH],
                                  start=(dc == 0), stop=(dc == DC - 1)),
                         inc=("T", 1) if dc == DC - 1 else None, wait=w0)
                tu = k.cnt["T"]
                strip_t[ssem] = tu
                sg2 = k.op("scalar", lambda e, h=h:
                           e.activation(out=SG[:], in_=psA(h), func=AF.Silu),
                           inc=("S", 1),
                           wait=[("T", tg), ("V", z_hist.get((th, hc - 1), 0))])
                silu_hist[(th, hc)] = sg2
                zv = k.op("vector", lambda e, hc=hc, h=h:
                          e.tensor_tensor(Z[:, hc * TH:(hc + 1) * TH], psB(h), SG[:],
                                          ALU.mult),
                          inc=("V", 1), wait=[("T", tu), ("S", sg2)])
                z_hist[(th, hc)] = zv
            zall = k.cnt["V"]
            for dmc in range(DC):
                slot = dmc % 2
                ssem = "wsA" if slot == 0 else "wsB"
                wcc = [("cc", cc_d)] if (first_app and th == 0 and dmc < 2) else []
                ld = k.op("sync", lambda e, dmc=dmc, slot=slot:
                          e.dma_start(out=wstr[slot][:, 0:H],
                                      in_=pd_all[dmc * 128:(dmc + 1) * 128, :]),
                          inc=(ssem, 16), wait=wcc + [("T", strip_t[ssem])])
                h = dmc % 2
                for hcc in range(HC):
                    w0 = ([(ssem, ld), ("V", zall), ("V", k.cnt["V"])]
                          if hcc == 0 else None)
                    k.op("tensor", lambda e, slot=slot, hcc=hcc, h=h:
                         e.matmul(psC(h), wstr[slot][:, hcc * 128:(hcc + 1) * 128],
                                  Z[:, hcc * TH:(hcc + 1) * TH],
                                  start=(hcc == 0), stop=(hcc == HC - 1)),
                         inc=("T", 1) if hcc == HC - 1 else None, wait=w0)
                td = k.cnt["T"]
                strip_t[ssem] = td
                k.op("vector", lambda e, dmc=dmc, h=h, th=th:
                     e.tensor_add(XT[dmc][:, th * TH:(th + 1) * TH], psC(h),
                                  XT[dmc][:, th * TH:(th + 1) * TH]),
                     inc=("V", 1), wait=[("T", td)])

    if STAGE >= 1:
        sc_f1 = contextlib.ExitStack()
        wstr1 = [sc_f1.enter_context(sb(f"w1s{i}", [128, H], BF16)) for i in range(2)]
        Z1 = sc_f1.enter_context(sb("Z1", [128, HC * TH], BF16))
        SG1 = sc_f1.enter_context(sb("SG1", [128, TH], BF16))
        emit_ffn(True, wstr1, Z1, SG1)
        sc_f1.close()

    if STAGE >= 2:
        sc2 = contextlib.ExitStack()
        idxs = sc2.enter_context(sb("idxs", [128, 192], I32))
        Mneg = sc2.enter_context(sb("Mneg", [128, 32], F32))
        Bsat = sc2.enter_context(sb("Bsat", [128, 2048], F32))
        band = sc2.enter_context(sb("band", [128, NBAND * 512], BF16))

        scD = contextlib.ExitStack()
        kstage = [scD.enter_context(sb(f"kst{i}", [128, TH], BF16)) for i in range(2)]
        vstage = [scD.enter_context(sb(f"vst{i}", [128, TH], BF16)) for i in range(2)]
        vtt = [scD.enter_context(sb(f"vtt{i}", [128, 128], BF16)) for i in range(2)]
        nstage = scD.enter_context(sb("nstage", [1, TH], F32))
        nsq = scD.enter_context(sb("nsq", [128, TH], BF16))
        wstrA = [scD.enter_context(sb(f"wsa{i}", [128, 2048], BF16)) for i in range(2)]
        vr = [scD.enter_context(sb(f"vr{i}", [128, D], BF16)) for i in range(4)]
        gkbuf = scD.enter_context(sb("gkbuf", [32, 2048], F32))
        gk2 = scD.enter_context(sb("gk2", [32, 1], F32))
        gq2 = scD.enter_context(sb("gq2", [32, 1], F32))
        gtmp = scD.enter_context(sb("gtmp", [32, 1], F32))
        mrow = scD.enter_context(sb("mrow", [1, 32], F32))
        bmax_bc = scD.enter_context(sb("bmax_bc", [32, 1], F32))
        brow_sat = scD.enter_context(sb("brow_sat", [1, 2048], F32))

        strip_t = {"wsA": k.cnt["T"], "wsB": k.cnt["T"]}
        ev_q = 0
        ev_v = 0
        vstage_t = [0, 0]
        vtt_g = [0, 0]
        kst_g = [0, 0]
        qkv_entry_V = k.cnt["V"]
        for th in range(2):
            lnv = emit_ln(th)
            for w in range(3):
                for ec in range(ECN):
                    slot = ec % 2
                    ssem = "wsA" if slot == 0 else "wsB"
                    r, l = ec // 2, ec % 2
                    row = (r * 4 + w) * 256 + l * 128
                    wcc = [("cc", cc_a)] if (th == 0 and w == 0 and ec < 2) else []
                    lw = k.op("sync", lambda e, row=row, slot=slot:
                              e.dma_start(out=wstrA[slot][:], in_=pa_all[row:row + 128, :]),
                              inc=(ssem, 16), wait=wcc + [("T", strip_t[ssem])])
                    h = ec % 2
                    for dc in range(DC):
                        w0 = None
                        if dc == 0:
                            w0 = [(ssem, lw), ("V", lnv), ("S", ev_q), ("V", ev_v)]
                            if th == 0 and w == 0 and ec < 2:
                                w0.append(("V", qkv_entry_V))
                        k.op("tensor", lambda e, slot=slot, dc=dc, h=h:
                             e.matmul(psA(h), wstrA[slot][:, dc * 128:(dc + 1) * 128],
                                      LN1[dc][:, 0:TH],
                                      start=(dc == 0), stop=(dc == DC - 1)),
                             inc=("T", 1) if dc == DC - 1 else None, wait=w0)
                    tq = k.cnt["T"]
                    if w < 2:
                        qk_dst = q_dram if w == 0 else kt_in
                        nrm_dst = qn_dram if w == 0 else kn_in
                        kss = "ksA" if slot == 0 else "ksB"
                        ev_q = k.op("scalar", lambda e, slot=slot, h=h, kss=kss:
                                    e.activation(out=kstage[slot][:], in_=psA(h),
                                                 func=AF.Copy),
                                    inc=("S", 1), wait=[("T", tq), (kss, kst_g[slot])])
                        kst_g[slot] = k.op(
                            "gpsimd", lambda e, ec=ec, th=th, slot=slot, qk_dst=qk_dst:
                            e.dma_start(out=qk_dst[ec * 128:(ec + 1) * 128,
                                                   th * TH:(th + 1) * TH],
                                        in_=kstage[slot][:]),
                            inc=(kss, 16), wait=[("S", ev_q)])
                        sq2 = k.op("scalar", lambda e, slot=slot:
                                   e.activation(out=nsq[:], in_=kstage[slot][:],
                                                func=AF.Square),
                                   inc=("S", 1), wait=[("T", k.cnt["T"]), ("S", ev_q)])
                    if w < 2:
                        for hh in range(2):
                            mask = mask_top if hh == 0 else mask_bot
                            head = ec * 2 + hh
                            tn = k.op("tensor", lambda e, mask=mask:
                                      e.matmul(ps_d[0:1, 0:TH], mask[:], nsq[:],
                                               start=True, stop=True),
                                      inc=("T", 1), wait=[("S", sq2), ("V", k.cnt["V"])])
                            vn = k.op("vector", lambda e:
                                      e.tensor_copy(nstage[:], ps_d[0:1, 0:TH]),
                                      inc=("V", 1), wait=[("T", tn), ("nst", k.cnt["nst"])])
                            k.op("gpsimd", lambda e, head=head, th=th, nd=nrm_dst:
                                 e.dma_start(out=nd[head:head + 1, th * TH:(th + 1) * TH],
                                             in_=nstage[:]),
                                 inc=("nst", 16), wait=[("V", vn)])
                    if w == 2:
                        ev_v = k.op("vector", lambda e, slot=slot, h=h:
                                    e.tensor_copy(vstage[slot][:], psA(h)),
                                    inc=("V", 1), wait=[("T", tq), ("T", vstage_t[slot])])
                        for tt in range(4):
                            h2 = tt % 2
                            tv = k.op("tensor", lambda e, slot=slot, tt=tt, h2=h2:
                                      e.transpose(psC(h2, 64).bitcast(BF16),
                                                  vstage[slot][:, tt * 128:(tt + 1) * 128],
                                                  idb[:]),
                                      inc=("T", 1), wait=[("V", ev_v), ("V", k.cnt["V"])])
                            vstage_t[slot] = tv
                            ti = tt % 2
                            vts = "vtA" if ti == 0 else "vtB"
                            vt = k.op("vector", lambda e, ti=ti, h2=h2, vts=vts:
                                      e.tensor_copy(vtt[ti][:], psC(h2, 64).bitcast(BF16)),
                                      inc=("V", 1), wait=[("T", tv), (vts, vtt_g[ti])])
                            vtt_g[ti] = k.op(
                                "gpsimd",
                                lambda e, ec=ec, th=th, tt=tt, ti=ti:
                                e.dma_start(out=v_in[th * TH + tt * 128:
                                                     th * TH + (tt + 1) * 128,
                                                     ec * 128:(ec + 1) * 128],
                                            in_=vtt[ti][:]),
                                inc=(vts, 16), wait=[("V", vt)])
                    strip_t[ssem] = k.cnt["T"]
        qkv_bar = [("ksA", k.cnt["ksA"]), ("ksB", k.cnt["ksB"]),
                   ("vtA", k.cnt["vtA"]), ("vtB", k.cnt["vtB"]),
                   ("nst", k.cnt["nst"])]
        cc_kt = k.op("gpsimd", lambda e: e.collective_compute(
            "AllGather", ALU.bypass, replica_groups=[list(range(NC))],
            ins=[kt_in.ap().opt()], outs=[kt_out.ap().opt()]),
            inc=("cc", 1), wait=qkv_bar)
        cc_v = k.op("gpsimd", lambda e: e.collective_compute(
            "AllGather", ALU.bypass, replica_groups=[list(range(NC))],
            ins=[v_in.ap().opt()], outs=[v_out.ap().opt()]),
            inc=("cc", 1))
        cc_kn = k.op("gpsimd", lambda e: e.collective_compute(
            "AllGather", ALU.bypass, replica_groups=[list(range(NC))],
            ins=[kn_in.ap().opt()], outs=[kn_out.ap().opt()]),
            inc=("cc", 1))

        ld_idx = k.op("sync", lambda e: e.dma_start(out=idxs[:], in_=idx_tab[:, :]),
                      inc=("idx", 16))
        vr_store = [0, 0, 0, 0]
        for kc in range(KCN):
            slot = kc % 4
            vrs = ("vrA", "vrB", "vrC", "vrD")[slot]
            gv = k.op("gpsimd", lambda e, kc=kc, slot=slot:
                      e.indirect_dma_start(
                          vr[slot][:, :], None, v_out[:, :],
                          IndirectOffsetOnAxis(idxs[:, 128 + kc:129 + kc], 0)),
                      inc=(vrs, 16),
                      wait=([("cc", cc_v), ("idx", ld_idx)] if kc == 0
                            else [(vrs, vr_store[slot])]))
            vr_store[slot] = k.op(
                "gpsimd", lambda e, kc=kc, slot=slot:
                e.dma_start(out=v_rot[kc * 128:(kc + 1) * 128, :], in_=vr[slot][:]),
                inc=(vrs, 16), wait=[(vrs, gv)])
        vrot_bar = [("vrA", k.cnt["vrA"]), ("vrB", k.cnt["vrB"]),
                    ("vrC", k.cnt["vrC"]), ("vrD", k.cnt["vrD"])]

        vk = k.op("vector", lambda e: e.memset(gk2[:], 0.0), inc=("V", 1))
        for part in range(4):
            lkn = k.op("sync", lambda e, part=part: e.dma_start(
                out=gkbuf[:].rearrange("h (r t) -> h r t", r=2),
                in_=dview(kn_out, part * 2 * HEADS * T,
                          [[T, 32], [HEADS * T, 2], [1, T]])),
                inc=("gkb", 16), wait=[("cc", cc_kn), ("V", vk)])
            vred = k.op("vector", lambda e: e.reduce_max(gtmp[:], gkbuf[:], AX.X),
                        inc=("V", 1), wait=[("gkb", lkn)])
            vk = k.op("vector", lambda e: e.tensor_max(gk2[:], gk2[:], gtmp[:]),
                      inc=("V", 1), wait=[("V", vred)])
        lqn = k.op("sync", lambda e: e.dma_start(
            out=gkbuf[:, 0:T], in_=qn_dram[:, :]),
            inc=("gkb", 16), wait=[("V", vk)] + qkv_bar)
        vq = k.op("vector", lambda e: e.reduce_max(gq2[:], gkbuf[:, 0:T], AX.X),
                  inc=("V", 1), wait=[("gkb", lqn)])
        tb = k.op("tensor", lambda e: e.matmul(ps_d[0:32, 0:1], ones_row_f[:, 0:32],
                                               bmax_sb[:], start=True, stop=True),
                  inc=("T", 1), wait=[("V", relprep_V), ("V", k.cnt["V"])])
        vbm = k.op("vector", lambda e: e.tensor_copy(bmax_bc[:], ps_d[0:32, 0:1]),
                   inc=("V", 1), wait=[("T", tb)])
        vs1 = k.op("vector", lambda e: e.tensor_add(gtmp[:], gq2[:], gk2[:]), inc=("V", 1),
                   wait=[("V", vq), ("V", vbm)])
        vM = k.op("vector", lambda e: e.scalar_tensor_tensor(
            gq2[:], gtmp[:], -1.0 / 16, bmax_bc[:], ALU.mult, ALU.subtract),
            inc=("V", 1), wait=[("V", vs1)])
        tM = k.op("tensor", lambda e: e.transpose(ps_d[0:1, 0:32], gq2[:],
                                                  idf[0:32, 0:32]),
                  inc=("T", 1), wait=[("V", vM)])
        vMr = k.op("vector", lambda e: e.tensor_copy(mrow[:], ps_d[0:1, 0:32]),
                   inc=("V", 1), wait=[("T", tM)])
        tMn = k.op("tensor", lambda e: e.matmul(ps_d[:, 0:32], ones_row_f[:], mrow[:],
                                                start=True, stop=True),
                   inc=("T", 1), wait=[("V", vMr)])
        k.op("vector", lambda e: e.tensor_copy(Mneg[:], ps_d[:, 0:32]),
             inc=("V", 1), wait=[("T", tMn)])
        for hh in range(HEADS):
            k.op("vector", lambda e, hh=hh: e.tensor_scalar_add(
                fview(brow_sat, hh * 64, [[32, 2], [1, 32]]),
                fview(brow_ps, NP_P, [[64, 2], [2, 32]]),
                mrow[0:1, hh:hh + 1]), inc=("V", 1),
                wait=[("V", vMr)] if hh == 0 else None)
        for n0 in range(0, 2048, 512):
            h = (n0 // 512) % 2
            tB2 = k.op("tensor", lambda e, n0=n0, h=h:
                       e.matmul(psD(h), ones_row_f[:], brow_sat[0:1, n0:n0 + 512],
                                start=True, stop=True),
                       inc=("T", 1), wait=[("V", k.cnt["V"])])
            k.op("vector", lambda e, n0=n0, h=h:
                 e.tensor_copy(Bsat[:, n0:n0 + 512], psD(h)),
                 inc=("V", 1), wait=[("T", tB2)])

        if DEBUG:
            k.op("gpsimd", lambda e: e.dma_start(out=dbg_m.ap().ap and dbg_m[0:128, 0:32], in_=Mneg[:]),
                 inc=("oqA", 16), wait=[("V", k.cnt["V"])])
            k.op("gpsimd", lambda e: e.dma_start(out=dbg_m[0:32, 32:33], in_=gk2[:], allow_slow_non_contiguous=True),
                 inc=("oqA", 16))
            k.op("gpsimd", lambda e: e.dma_start(out=dbg_m[0:32, 33:34], in_=gq2[:], allow_slow_non_contiguous=True),
                 inc=("oqA", 16))
            k.op("gpsimd", lambda e: e.dma_start(out=dbg_m[0:1, 34:35], in_=bmax_sb[:]),
                 inc=("oqA", 16))
            k.op("gpsimd", lambda e: e.dma_start(out=dbg_m[0:128, 35:67], in_=Bsat[:, 0:32]),
                 inc=("oqA", 16))
        for p in range(128):
            k.op("sync", lambda e, p=p: e.dma_start(
                out=band[p:p + 1, :].rearrange("o (t j) -> o t j", t=NBAND),
                in_=dview(bexp_dram, 127 - p, [[0, 1], [640, NBAND], [1, 512]])),
                inc=("sq", 16), wait=[("bx", relprep_gq)] if p == 0 else None)
        band_sq = k.cnt["sq"]
        bias_V = k.cnt["V"]
        scD.close()

        scE = contextlib.ExitStack()
        KP2 = [scE.enter_context(sb(f"KP{i}", [128, S], BF16)) for i in range(2)]
        QT2 = [scE.enter_context(sb(f"QTs{i}", [128, T], BF16)) for i in range(2)]
        VP2 = [scE.enter_context(sb(f"VPs{i}", [128, KCN * 130], BF16)) for i in range(2)]
        EA = scE.enter_context(sb("EA", [128, 1024], BF16))
        EB = scE.enter_context(sb("EB", [128, 1024], BF16))
        ctstg = [scE.enter_context(sb(f"cts{i}", [65, 128], F32)) for i in range(2)]
        ctn = scE.enter_context(sb("ctn", [128, 65], F32))
        rdc = scE.enter_context(sb("rdc", [128, 1], F32))
        ctxn = [scE.enter_context(sb(f"ctxn{i}", [128, 64], BF16)) for i in range(2)]

        vone = k.op("vector", lambda e: e.memset(VP2[0][:], 1.0), inc=("V", 1),
                    wait=[("T", k.cnt["T"])] + vrot_bar + [("cc", cc_kt)])
        vone = k.op("vector", lambda e: e.memset(VP2[1][:], 1.0), inc=("V", 1))

        seA_h = {}
        seB_h = {}
        pvA_h = {}
        pvB_h = {}
        prev_evac = 0
        ctx_dma = [0, 0]
        pair_T_end = {}
        loads = {}

        def load_pair(pair, wait_t):
            sl = pair % 2
            kps = "kp" if sl == 0 else "kp2"
            vps = "vp" if sl == 0 else "vp2"
            kp_t = 0
            for u in range(8):
                kp_t = k.op("gpsimd", lambda e, u=u, pair=pair, sl=sl:
                            e.indirect_dma_start(
                                KP2[sl][:, u * 1024:(u + 1) * 1024], None, kt_out[:, :],
                                IndirectOffsetOnAxis(
                                    idxs[:, u * 16 + pair:u * 16 + pair + 1], 0)),
                            inc=(kps, 16),
                            wait=([("cc", cc_kt), ("idx", ld_idx)] + vrot_bar
                                  if pair == 0 and u == 0 else
                                  ([("T", wait_t)] if u == 0 else None)))
            k.op("sync", lambda e, pair=pair, sl=sl: e.dma_start(
                out=fview(VP2[sl], 0, [[130, KCN], [1, 64]]),
                in_=dview(v_rot, 128 * pair,
                          [[D, 128], [128 * D, KCN], [1, 64]])),
                inc=(vps, 16),
                wait=(vrot_bar + [("V", vone), ("sq", band_sq)] if pair <= 1
                      else [("T", wait_t)]))
            k.op("sync", lambda e, pair=pair, sl=sl: e.dma_start(
                out=fview(VP2[sl], 65, [[130, KCN], [1, 64]]),
                in_=dview(v_rot, 128 * pair + 64,
                          [[D, 128], [128 * D, KCN], [1, 64]])),
                inc=(vps, 16))
            vp_t = k.op("sync", lambda e, pair=pair, sl=sl: e.dma_start(
                out=QT2[sl][:], in_=q_dram[pair * 128:(pair + 1) * 128, :]),
                inc=(vps, 16),
                wait=(qkv_bar + vrot_bar) if pair == 0 else None)
            return (kps, kp_t, vps, vp_t)

        loads[0] = load_pair(0, 0)

        for pair in range(PAIRS):
            sl = pair % 2
            KP = KP2[sl]
            QTs = QT2[sl]
            VP = VP2[sl]
            if pair + 1 < PAIRS:
                loads[pair + 1] = load_pair(pair + 1, pair_T_end.get(pair - 1, 0))
            kps, kp_t, vps, vp_t = loads[pair]

            def emit_pv(kc, wA, wB, VP=VP):
                pvA_h[kc] = k.op(
                    "tensor", lambda e, kc=kc, VP=VP:
                    e.matmul(ps_c[0:65, 0:TH], VP[:, kc * 130: kc * 130 + 65],
                             EA[:, (kc % 2) * 512: (kc % 2) * 512 + TH],
                             start=(kc == 0), stop=(kc == KCN - 1)),
                    inc=("T", 1),
                    wait=wA + ([("V", prev_evac)] if kc == 0 else []))
                pvB_h[kc] = k.op(
                    "tensor", lambda e, kc=kc, VP=VP:
                    e.matmul(ps_c[0:65, 512:512 + TH], VP[:, kc * 130 + 65: kc * 130 + 130],
                             EB[:, (kc % 2) * 512: (kc % 2) * 512 + TH],
                             start=(kc == 0), stop=(kc == KCN - 1)),
                    inc=("T", 1), wait=wB)

            for qt in range(2):
                pend = None
                for kc in range(KCN):
                    h = kc % 2
                    hA, hB = 2 * pair, 2 * pair + 1
                    is_mult = (qt, kc) in MIDX
                    m = MIDX.get((qt, kc))
                    kpr = kc // 2
                    w0 = [("S", seA_h.get(kc - 2, 0))]
                    if kc == 0 and qt == 0:
                        w0 += [(kps, kp_t), (vps, vp_t), ("V", bias_V)]
                        if pair == 0:
                            w0 += [("S", k.cnt["S"])]
                    tsA = k.op("tensor", lambda e, kc=kc, h=h, qt=qt, KP=KP, QTs=QTs:
                               e.matmul(psA(h), KP[0:64, kc * 128:(kc + 1) * 128],
                                        QTs[0:64, qt * TH:(qt + 1) * TH],
                                        start=True, stop=True, tile_position=(0, 0)),
                               inc=("T", 1), wait=w0)
                    tsB = k.op("tensor", lambda e, kc=kc, h=h, qt=qt, KP=KP, QTs=QTs:
                               e.matmul(psB(h), KP[64:128, kc * 128:(kc + 1) * 128],
                                        QTs[64:128, qt * TH:(qt + 1) * TH],
                                        start=True, stop=True, tile_position=(64, 0)),
                               inc=("T", 1), wait=[("S", seB_h.get(kc - 2, 0))])
                    if pend is not None:
                        emit_pv(*pend)
                    if is_mult:
                        biasA = Mneg[:, hA:hA + 1]
                        biasB = Mneg[:, hB:hB + 1]
                    else:
                        biasA = Bsat[:, hA * 64 + qt * 32 + kpr: hA * 64 + qt * 32 + kpr + 1]
                        biasB = Bsat[:, hB * 64 + qt * 32 + kpr: hB * 64 + qt * 32 + kpr + 1]
                    seA = k.op("scalar", lambda e, h=h, biasA=biasA:
                               e.activation(out=EA[:, h * 512: h * 512 + TH],
                                            in_=psA(h), func=AF.Exp,
                                            bias=biasA, scale=0.125),
                               inc=("S", 1),
                               wait=[("T", tsA), ("T", pvA_h.get(kc - 2, 0))])
                    seA_h[kc] = seA
                    seB = k.op("scalar", lambda e, h=h, biasB=biasB:
                               e.activation(out=EB[:, h * 512: h * 512 + TH],
                                            in_=psB(h), func=AF.Exp,
                                            bias=biasB, scale=0.125),
                               inc=("S", 1),
                               wait=[("T", tsB), ("T", pvB_h.get(kc - 2, 0))])
                    seB_h[kc] = seB
                    wA = [("S", seA)]
                    wB = [("S", seB)]
                    if is_mult:
                        bvA = k.op("vector", lambda e, h=h, m=m:
                                   e.tensor_tensor(EA[:, h * 512: h * 512 + TH],
                                                   EA[:, h * 512: h * 512 + TH],
                                                   band[:, m * 512:(m + 1) * 512],
                                                   ALU.mult),
                                   inc=("V", 1), wait=[("S", seA)])
                        bvB = k.op("vector", lambda e, h=h, m=m:
                                   e.tensor_tensor(EB[:, h * 512: h * 512 + TH],
                                                   EB[:, h * 512: h * 512 + TH],
                                                   band[:, m * 512:(m + 1) * 512],
                                                   ALU.mult),
                                   inc=("V", 1), wait=[("S", seB)])
                        wA = [("V", bvA)]
                        wB = [("V", bvB)]
                    pend = (kc, wA, wB)
                emit_pv(*pend)
                prev_evac = 0
                prev_vs = 0
                for hh in range(2):
                    poff = 0 if hh == 0 else 512
                    pvw = pvA_h[KCN - 1] if hh == 0 else pvB_h[KCN - 1]
                    for tq in range(4):
                        h2 = tq % 2
                        ci = tq % 2
                        cxs = "cxA" if ci == 0 else "cxB"
                        evw = k.op("vector", lambda e, poff=poff, tq=tq, ci=ci:
                                   e.tensor_copy(ctstg[ci][:],
                                                 ps_c[0:65, poff + tq * 128: poff + (tq + 1) * 128]),
                                   inc=("V", 1), wait=[("T", pvw), ("T", k.cnt["T"])])
                        prev_evac = evw
                        tv = k.op("tensor", lambda e, ci=ci, h2=h2:
                                  e.transpose(ps_d[:, h2 * 512: h2 * 512 + 65],
                                              ctstg[ci][:], idf[0:65, 0:65]),
                                  inc=("T", 1), wait=[("V", evw), ("V", k.cnt["V"])])
                        vc = k.op("vector", lambda e, h2=h2:
                                  e.tensor_copy(ctn[:], ps_d[:, h2 * 512: h2 * 512 + 65]),
                                  inc=("V", 1), wait=[("T", tv), ("V", prev_vs)])
                        vr2 = k.op("vector", lambda e:
                                   e.reciprocal(rdc[:], ctn[:, 64:65]),
                                   inc=("V", 1), wait=[("V", vc)])
                        vs = k.op("vector", lambda e, ci=ci:
                                  e.tensor_scalar(ctxn[ci][:], ctn[:, 0:64], rdc[:],
                                                  None, ALU.mult),
                                  inc=("V", 1),
                                  wait=[("V", vr2), (cxs, ctx_dma[ci])])
                        prev_vs = vs
                        ctx_dma[ci] = k.op(
                            "gpsimd",
                            lambda e, pair=pair, qt=qt, hh=hh, tq=tq, ci=ci:
                            e.dma_start(
                                out=ctx_dram[qt * TH + tq * 128: qt * TH + (tq + 1) * 128,
                                             pair * 128 + hh * 64: pair * 128 + hh * 64 + 64],
                                in_=ctxn[ci][:]),
                            inc=(cxs, 16), wait=[("V", vs)])
            pair_T_end[pair] = k.cnt["T"]
        attn_bar = [("cxA", k.cnt["cxA"]), ("cxB", k.cnt["cxB"])]
        scE.close()

        scF = contextlib.ExitStack()
        ctile = [scF.enter_context(sb(f"ctile{i}", [128, D], BF16)) for i in range(2)]
        ctxT = [scF.enter_context(sb(f"ctxT{i}", [128, T], BF16)) for i in range(DC)]
        wstrF = [scF.enter_context(sb(f"wsf{i}", [128, 2048], BF16)) for i in range(2)]
        ct_t = [k.cnt["T"], k.cnt["T"]]
        f_entry_V = k.cnt["V"]
        for tt in range(8):
            slot = tt % 2
            cts = "ctA" if slot == 0 else "ctB"
            lc = k.op("sync", lambda e, tt=tt, slot=slot:
                      e.dma_start(out=ctile[slot][:],
                                  in_=ctx_dram[tt * 128:(tt + 1) * 128, :]),
                      inc=(cts, 16),
                      wait=attn_bar + [("T", ct_t[slot])])
            for dc in range(DC):
                h2 = dc % 2
                tv = k.op("tensor", lambda e, slot=slot, dc=dc, h2=h2:
                          e.transpose(psC(h2, 64).bitcast(BF16),
                                      ctile[slot][:, dc * 128:(dc + 1) * 128], idb[:]),
                          inc=("T", 1),
                          wait=[(cts, lc), ("V", k.cnt["V"])]
                               + ([("V", f_entry_V)] if tt == 0 and dc == 0 else []))
                k.op("vector", lambda e, dc=dc, tt=tt, h2=h2:
                     e.tensor_copy(ctxT[dc][:, tt * 128:(tt + 1) * 128],
                                   psC(h2, 64).bitcast(BF16)),
                     inc=("V", 1), wait=[("T", tv)])
            ct_t[slot] = k.cnt["T"]
        ctxT_V = k.cnt["V"]
        strip_t = {"wsA": ct_t[0], "wsB": ct_t[1]}
        for fc in range(ECN):
            slot = fc % 2
            ssem = "wsA" if slot == 0 else "wsB"
            r, l = fc // 2, fc % 2
            row = (r * 4 + 3) * 256 + l * 128
            lw = k.op("sync", lambda e, row=row, slot=slot:
                      e.dma_start(out=wstrF[slot][:], in_=pa_all[row:row + 128, :]),
                      inc=(ssem, 16),
                      wait=[("T", strip_t[ssem])] + (attn_bar if fc < 2 else []))
            for th in range(2):
                h = (fc * 2 + th) % 2
                for ec in range(DC):
                    w0 = ([(ssem, lw), ("V", ctxT_V), ("V", k.cnt["V"])]
                          if ec == 0 else None)
                    k.op("tensor", lambda e, slot=slot, ec=ec, h=h, th=th:
                         e.matmul(psA(h), wstrF[slot][:, ec * 128:(ec + 1) * 128],
                                  ctxT[ec][:, th * TH:(th + 1) * TH],
                                  start=(ec == 0), stop=(ec == DC - 1)),
                         inc=("T", 1) if ec == DC - 1 else None, wait=w0)
                to = k.cnt["T"]
                k.op("vector", lambda e, fc=fc, th=th, h=h:
                     e.tensor_add(XT[fc][:, th * TH:(th + 1) * TH], psA(h),
                                  XT[fc][:, th * TH:(th + 1) * TH]),
                     inc=("V", 1), wait=[("T", to)])
            strip_t[ssem] = k.cnt["T"]
        scF.close()
        sc2.close()

    if STAGE >= 3:
        sc_f2 = contextlib.ExitStack()
        wstr2 = [sc_f2.enter_context(sb(f"w2s{i}", [128, H], BF16)) for i in range(2)]
        Z2 = sc_f2.enter_context(sb("Z2", [128, HC * TH], BF16))
        SG2 = sc_f2.enter_context(sb("SG2", [128, TH], BF16))
        emit_ffn(False, wstr2, Z2, SG2)
        sc_f2.close()

    ostage = [stack.enter_context(sb(f"os{i}", [128, D], F32)) for i in range(2)]
    out_dma = [0, 0]
    for tt in range(8):
        slot = tt % 2
        oqs = "oqA" if slot == 0 else "oqB"
        for dc in range(DC):
            h2 = dc % 2
            tv = k.op("tensor", lambda e, tt=tt, dc=dc, h2=h2:
                      e.transpose(psC(h2, 128), XT[dc][:, tt * 128:(tt + 1) * 128],
                                  idf[:]),
                      inc=("T", 1),
                      wait=[("V", k.cnt["V"]), (oqs, out_dma[slot])])
            k.op("vector", lambda e, slot=slot, dc=dc, h2=h2:
                 e.tensor_copy(ostage[slot][:, dc * 128:(dc + 1) * 128], psC(h2, 128)),
                 inc=("V", 1), wait=[("T", tv)])
        sv = k.cnt["V"]
        out_dma[slot] = k.op("gpsimd", lambda e, tt=tt, slot=slot:
                             e.dma_start(out=y_out[tt * 128:(tt + 1) * 128, :],
                                         in_=ostage[slot][:]),
                             inc=(oqs, 16), wait=[("V", sv)])
    if DEBUG and STAGE >= 2:
        for (dst, srct) in ((dbg_ctx, ctx_dram), (dbg_q, q_dram), (dbg_kt, kt_out),
                            (dbg_vr, v_rot), (dbg_kn, kn_out), (dbg_vin, v_in)):
            k.op("gpsimd", lambda e, dst=dst, srct=srct:
                 e.dma_start(out=dst.ap(), in_=srct.ap()), inc=("oqA", 16))
        k.op("gpsimd", lambda e:
             e.dma_start(out=dbg_bx.ap(), in_=bexp_dram.ap()), inc=("oqA", 16))
    k.wait("gpsimd", "oqA", k.cnt["oqA"])
    k.wait("gpsimd", "oqB", k.cnt["oqB"])

    with nc.Block() as block:
        @block.sync
        def _(e):
            for f in k.st["sync"]:
                f(e)

        @block.gpsimd
        def _(e):
            for f in k.st["gpsimd"]:
                f(e)

        @block.vector
        def _(e):
            for f in k.st["vector"]:
                f(e)

        @block.scalar
        def _(e):
            for f in k.st["scalar"]:
                f(e)

        @block.tensor
        def _(e):
            for f in k.st["tensor"]:
                f(e)

    stack.close()
    return nc


def kernel(L0_attn_hidden_states, L0_attn_wq, L0_attn_wk, L0_attn_wv,
           L0_attn_wo, L0_attn_rel_pos, L0_ffn_wq_gate, L0_ffn_s_gate,
           L0_ffn_wq_up, L0_ffn_s_up, L0_ffn_wq_down, L0_ffn_s_down):
    x = np.ascontiguousarray(np.asarray(L0_attn_hidden_states, dtype=np.float32))
    wq = np.asarray(L0_attn_wq, np.float32)
    wk = np.asarray(L0_attn_wk, np.float32)
    wv = np.asarray(L0_attn_wv, np.float32)
    wo = np.asarray(L0_attn_wo, np.float32)
    rel = np.asarray(L0_attn_rel_pos, np.float32)
    wg = np.asarray(L0_ffn_wq_gate, np.int32)
    sg = np.asarray(L0_ffn_s_gate, np.float32)
    wu = np.asarray(L0_ffn_wq_up, np.int32)
    su = np.asarray(L0_ffn_s_up, np.float32)
    wd = np.asarray(L0_ffn_wq_down, np.int32)
    sd = np.asarray(L0_ffn_s_down, np.float32)

    identf = np.eye(128, dtype=np.float32)
    identb = np.eye(128, dtype=ml_dtypes.bfloat16)

    def pad_rows(a, n):
        if a.shape[0] == n:
            return a
        out = np.zeros((n,) + a.shape[1:], a.dtype)
        out[:a.shape[0]] = a
        return out

    in_maps = []
    for c in range(NC):
        g0, g1 = 768 * c, min(768 * (c + 1), H)
        in_maps.append({
            "x_in": np.ascontiguousarray(x[0, T * c:T * (c + 1), :]),
            "wg_sl": pad_rows(np.ascontiguousarray(wg[g0:g1]), 768),
            "wu_sl": pad_rows(np.ascontiguousarray(wu[g0:g1]), 768),
            "wd_sl": np.ascontiguousarray(wd[256 * c:256 * (c + 1)]),
            "sg_sl": np.ascontiguousarray(pad_rows(sg[g0:g1], 768).reshape(HCP, 128).T),
            "su_sl": np.ascontiguousarray(pad_rows(su[g0:g1], 768).reshape(HCP, 128).T),
            "sd_sl": np.ascontiguousarray(sd[256 * c:256 * (c + 1)].reshape(2, 128).T),
            "wa_sl": np.ascontiguousarray(np.concatenate(
                [w[256 * c:256 * (c + 1)] for w in (wq, wk, wv, wo)], axis=0)),
            "rel_tabt": _host_rel_table(rel, c).astype(np.float32),
            "idx_tab": _host_idx_tables(c),
            "ident_f": identf,
            "ident_b": identb,
        })

    nc = build_nc()
    res = run_bass_kernel_spmd(nc, in_maps, core_ids=list(range(NC)))
    if DEBUG:
        DEBUG_OUT.clear()
        for key in res.results[0]:
            DEBUG_OUT[key] = [np.asarray(res.results[c][key]) for c in range(NC)]
    y = np.concatenate([res.results[c]["y"] for c in range(NC)], axis=0)
    return y[None, :, :].astype(np.float32)
